# version 09
# baseline (speedup 1.0000x reference)
import contextlib
import numpy as np
import concourse.bass as bass
import concourse.mybir as mybir
from concourse.bass_utils import run_bass_kernel_spmd

F32 = mybir.dt.float32
F16 = mybir.dt.float16
F8 = mybir.dt.float8e4
U8 = mybir.dt.uint8
AF = mybir.ActivationFunctionType
ALU = mybir.AluOpType
AX = mybir.AxisListType

D_MODEL = 1024
DH = 512
SEQ = 8192
BATCH = 4
NST = 16
DT_RANK = 64
HID = 4096
TC = 256
NCH = SEQ // TC
NT = NCH
NW1 = HID // 128
NW2 = D_MODEL // 128
NWT = NW1 + NW2
EPS = 1e-5
LAG = 3
NSLOT = NCH + LAG + 3


def build_fused(dbg=False):
    nc = bass.Bass("TRN2", target_bir_lowering=False, detect_race_conditions=False)
    x_h = nc.dram_tensor("x", [SEQ, D_MODEL], F32, kind="ExternalInput").ap()
    wcat_h = nc.dram_tensor("wcat_d", [D_MODEL, 96], F32, kind="ExternalInput").ap()
    wdt_h = nc.dram_tensor("wdt_d", [DT_RANK, DH], F32, kind="ExternalInput").ap()
    bdt_h = nc.dram_tensor("bdt_d", [DH], F32, kind="ExternalInput").ap()
    alog_h = nc.dram_tensor("alog_d", [DH, NST], F32, kind="ExternalInput").ap()
    dvec_h = nc.dram_tensor("dvec_d", [DH], F32, kind="ExternalInput").ap()
    identf_h = nc.dram_tensor("identf_d", [128, 128], F32, kind="ExternalInput").ap()
    identh_h = nc.dram_tensor("identh_d", [128, 128], F16, kind="ExternalInput").ap()
    hmask_h = nc.dram_tensor("hmask_d", [DH], F32, kind="ExternalInput").ap()
    w1_h = nc.dram_tensor("w1", [HID, D_MODEL], F32, kind="ExternalInput").ap()
    w2_h = nc.dram_tensor("w2", [D_MODEL, HID], F32, kind="ExternalInput").ap()
    out_h = nc.dram_tensor("out", [NT * 128, D_MODEL], F32, kind="ExternalOutput").ap()
    bc_h = nc.dram_tensor("bc_scratch", [2, NST, NCH, TC], F32, kind="Internal").ap()
    a2i_h = nc.dram_tensor("a2a_in", [NCH, 2, 128, DH], F32, kind="Internal").ap()
    a2o_h = nc.dram_tensor("a2a_out", [NCH, 2, 2, 128, DH], F32, kind="Internal").ap()
    sc_h = nc.dram_tensor("sc_scratch", [4], F32, kind="Internal").ap()
    if dbg in (True, 1, 2, 5, 7):
        dbg_r1b_h = nc.dram_tensor("dbg_r1b", [NT, 128, D_MODEL], F32,
                                   kind="ExternalOutput").ap()
        dbg_abuf_h = nc.dram_tensor("dbg_abuf", [NT, 128, 2, DH], F32,
                                    kind="ExternalOutput").ap()
        sdbga = nc.semaphore("sdbga").__enter__()
        sdbgr = nc.semaphore("sdbgr").__enter__()

    es = contextlib.ExitStack()
    with es:
        sb = lambda n, s, d=F32: es.enter_context(nc.sbuf_tensor(n, s, d))
        psa = lambda n, s, d=F32: es.enter_context(nc.psum_tensor(n, s, d))
        sem = lambda n: es.enter_context(nc.semaphore(n))

        wcat = sb("wcat", [128, 8, 96])
        wdt = sb("wdt", [64, 4, 128])
        bdt = sb("bdt", [128, 4])
        aneg = sb("aneg", [128, 4, NST])
        dplus = sb("dplus", [128, 4])
        identf = sb("identf", [128, 128])
        identh = sb("identh", [128, 128], F16)
        cm1 = sb("cm1", [128, 1])
        cn05 = sb("cn05", [128, 1])
        onescol = sb("onescol", [128, 1])
        sval = sb("sval", [128, 4])
        svsmall = sb("svsmall", [2, 2])
        acc2 = sb("acc2", [128, 2])
        w1T = sb("w1T", [128, 8, HID], F8)
        w2T = sb("w2T", [128, 32, D_MODEL], F8)
        xbuf = sb("xbuf", [128, 2, D_MODEL])
        xT = sb("xT", [128, 2, 8, TC])
        preSB = sb("preSB", [128, 2, TC])
        delta = sb("delta", [128, 2, 4, TC])
        uu = sb("uu", [128, 4, TC])
        e0 = sb("e0", [128, 2, 4, TC])
        dbx = sb("dbx", [128, 4, TC])
        hbuf = sb("hbuf", [128, 4, TC])
        brep = sb("brep", [128, 2, TC])
        crep = sb("crep", [128, 2, TC])
        tmpy = sb("tmpy", [128, 2, TC])
        yacc = sb("yacc", [128, 4, TC])
        hstate = sb("hstate", [128, 4, NST])
        r1T = sb("r1T", [128, 4, TC])
        r1o = sb("r1o", [128, 2, DH])
        r1b = sb("r1b", [128, 3, D_MODEL])
        cent = sb("cent", [128, D_MODEL])
        sq = sb("sq", [128, D_MODEL])
        gb = sb("gb", [128, HID])
        hqh = sb("hqh", [128, HID], F16)
        hqT = sb("hqT", [128, 32, 128], F16)
        xqh = sb("xqh", [128, D_MODEL], F16)
        abuf = sb("abuf", [128, 2, DH])
        hmaskf = sb("hmaskf", [128, 4])
        hmask = sb("hmask", [128, 1], U8)
        outb = sb("outb", [128, D_MODEL])
        stat = sb("stat", [128, 4, 12])
        wload = gb
        wqh = hqh
        xqT = hqT[:, 24:32]

        ps_m1 = psa("ps_m1", [128, 2, 512])
        ps_m2 = psa("ps_m2", [128, 2, 512])
        ps_tf = psa("ps_tf", [128, 2, 4, 128])
        ps_th = psa("ps_th", [128, 2, 8, 128], F16)

        sld = sem("sld")
        sprep = sem("sprep")
        ssfx = sem("ssfx")
        swl = sem("swl")
        swst = sem("swst")
        smms = sem("smms")
        sscal = sem("sscal")
        sbs = sem("sbs")
        swq = sem("swq")
        sth = sem("sth")
        seh = sem("seh")
        sxp = [sem("sx0"), sem("sx1")]
        stf = sem("stf")
        sef = sem("sef")
        smm1 = sem("smm1")
        spre = sem("spre")
        smmd = sem("smmd")
        ssp = sem("ssp")
        sdel = sem("sdel")
        se0 = sem("se0")
        ssc = sem("ssc")
        sdbx = sem("sdbx")
        scml = sem("scml")
        sbrp = [sem("sbr0"), sem("sbr1")]
        sbw = sem("sbw")
        sr1v = sem("sr1v")
        srwp = [sem("srw0"), sem("srw1")]
        sa2a = sem("sa2a")
        sxl = [sem("sxl0"), sem("sxl1"), sem("sxl2")]
        svar = sem("svar")
        srstd = sem("srstd")
        sxq = sem("sxq")
        sgsc = sem("sgsc")
        sm1 = sem("sm1")
        sg = sem("sg")
        shq = sem("shq")
        sm2 = sem("sm2")
        svar2 = sem("svar2")
        srstd2 = sem("srstd2")
        sr2 = sem("sr2")
        sowp = sem("sow")

        block = es.enter_context(nc.Block())

        def tile_A(s):
            i = s - LAG
            return i if 0 <= i < NT else None

        def tile_C(s):
            i = s - LAG - 1
            return i if 0 <= i < NT else None

        def tile_E(s):
            i = s - LAG - 2
            return i if 0 <= i < NT else None

        m1_seq = [("stats",)]
        for s in range(NSLOT):
            if s < NCH:
                m1_seq.append(("pre", s))
                for dt in range(4):
                    m1_seq.append(("dlt", s, dt))
            i = tile_A(s)
            if i is not None:
                for hc in range(8):
                    m1_seq.append(("ffn", i, hc))
        m1_par = {u: k % 2 for k, u in enumerate(m1_seq)}

        def m1_consumer(u):
            if u[0] == "stats":
                return (sscal, 1)
            if u[0] == "pre":
                return (spre, u[1] + 1)
            if u[0] == "dlt":
                return (ssp, 4 * u[1] + u[2] + 1)
            return (sg, 8 * u[1] + u[2] + 1)

        m1_prev = {}
        for k, u in enumerate(m1_seq):
            if k >= 2:
                m1_prev[u] = m1_consumer(m1_seq[k - 2])

        f32_seq = []
        for s in range(NSLOT):
            if s < NCH:
                for tt in range(2):
                    for d8 in range(8):
                        f32_seq.append(("x", s, tt, d8))
            if 1 <= s <= NCH:
                for tt in range(2):
                    for dt in range(4):
                        f32_seq.append(("r", s - 1, tt, dt))
        f32_idx = {u: k for k, u in enumerate(f32_seq)}
        assert len(f32_seq) % 4 == 0

        f16_seq = []
        for j in range(NWT):
            for k in range(8 if j < NW1 else 32):
                f16_seq.append(("w", j, k))
        for i in range(NT):
            for k in range(8):
                f16_seq.append(("xq", i, k))
            for k in range(32):
                f16_seq.append(("hq", i, k))
        f16_idx = {u: k for k, u in enumerate(f16_seq)}
        assert len(f16_seq) % 8 == 0

        @block.sync
        def _(sync):
            sync.dma_start(out=wcat[:], in_=wcat_h.rearrange("(a p) c -> p a c", p=128)).then_inc(sld, 16)
            sync.dma_start(out=wdt[:], in_=wdt_h.rearrange("r (a p) -> r a p", p=128)).then_inc(sld, 16)
            sync.dma_start(out=bdt[:], in_=bdt_h.rearrange("(a p) -> p a", p=128),
                           allow_slow_non_contiguous=True).then_inc(sld, 16)
            sync.dma_start(out=aneg[:], in_=alog_h.rearrange("(a p) n -> p a n", p=128)).then_inc(sld, 16)
            sync.dma_start(out=dplus[:], in_=dvec_h.rearrange("(a p) -> p a", p=128),
                           allow_slow_non_contiguous=True).then_inc(sld, 16)
            sync.dma_start(out=identf[:], in_=identf_h).then_inc(sld, 16)
            sync.dma_start(out=identh[:], in_=identh_h).then_inc(sld, 16)
            sync.dma_start(out=hmaskf[:], in_=hmask_h.rearrange("(a p) -> p a", p=128),
                           allow_slow_non_contiguous=True).then_inc(sld, 16)
            for j in range(NWT):
                if j >= 1:
                    sync.wait_ge(swst, j)
                src = (w1_h[j * 128:(j + 1) * 128, :] if j < NW1
                       else w2_h[(j - NW1) * 128:(j - NW1 + 1) * 128, :])
                sync.dma_start(out=(wload[:, 0:D_MODEL] if j < NW1 else wload[:, :]),
                               in_=src).then_inc(swl, 16)
            sync.wait_ge(sscal, 1)
            sync.dma_start(out=sc_h.rearrange("(a b) -> a b", a=2),
                           in_=svsmall[:, :]).then_inc(sbs, 16)
            sync.wait_ge(sbs, 16)
            sync.dma_start(out=sval[:, :],
                           in_=sc_h.unsqueeze(0).broadcast_to([128, 4])).then_inc(sbs, 16)
            sync.wait_ge(swst, NWT)
            for j in range(NWT):
                if j >= 1:
                    sync.wait_ge(swq, j)
                src = (w1_h[j * 128:(j + 1) * 128, :] if j < NW1
                       else w2_h[(j - NW1) * 128:(j - NW1 + 1) * 128, :])
                sync.dma_start(out=(wload[:, 0:D_MODEL] if j < NW1 else wload[:, :]),
                               in_=src).then_inc(swl, 16)
            for s in range(NSLOT):
                c = s
                if c < NCH:
                    for tt in range(2):
                        g = 2 * c + tt
                        if g >= 2:
                            sync.wait_ge(stf, f32_idx[("x", (g - 2) // 2, (g - 2) % 2, 7)] + 1)
                        if tt == (c - 1) % 2:
                            if c >= 1:
                                sync.wait_ge(sdel, 4 * c)
                        elif c >= 2:
                            sync.wait_ge(sdel, 4 * (c - 1))
                        sync.dma_start(
                            out=xbuf[:, g % 2],
                            in_=x_h[g * 128:(g + 1) * 128, :],
                        ).then_inc(sxp[g % 2], 16)
                k = tile_E(s - 1)
                if k is not None:
                    sync.wait_ge(sr2, k + 1)
                    sync.dma_start(out=out_h[k * 128:(k + 1) * 128, :],
                                   in_=outb[:]).then_inc(sowp, 16)
                if 1 <= s <= NCH:
                    cc = s - 1
                    sync.wait_ge(sbw, 16 * (cc + 1))
                    for n in range(NST):
                        gi = NST * cc + n
                        if gi >= 2:
                            pc, pn = divmod(gi - 2, NST)
                            sync.wait_ge(sdbx, 16 * pc + pn + 1)
                            sync.wait_ge(scml, 16 * pc + pn + 1)
                        sync.dma_start(
                            out=brep[:, n % 2],
                            in_=bc_h[0, n, cc, :].unsqueeze(0).broadcast_to([128, TC]),
                        ).then_inc(sbrp[n % 2], 16)
                        sync.dma_start(
                            out=crep[:, n % 2],
                            in_=bc_h[1, n, cc, :].unsqueeze(0).broadcast_to([128, TC]),
                        ).then_inc(sbrp[n % 2], 16)
                if c < NCH:
                    sync.wait_ge(spre, c + 1)
                    sync.dma_start(out=bc_h[:, :, c, :],
                                   in_=preSB[64:96, c % 2]).then_inc(sbw, 16)
                if 1 <= s <= NCH:
                    sync.wait_ge(sef, f32_idx[("r", cc, 1, 3)] // 4 + 1)
                    sync.dma_start(out=a2i_h[cc].rearrange("b t d -> t b d"),
                                   in_=r1o[:, :, :]).then_inc(srwp[cc % 2], 16)
                i = tile_A(s + 1)
                if i is not None:
                    sync.wait_ge(sa2a, i + 1)
                    if i >= 3:
                        sync.wait_ge(svar2, i - 2)
                    if i >= 1:
                        sync.wait_ge(svar, i)
                    sync.dma_start(out=r1b[:, i % 3, 0:DH],
                                   in_=a2o_h[i, 0, 0, :, :]).then_inc(sxl[i % 3], 16)
                    sync.dma_start(out=abuf[:, 0, :],
                                   in_=a2o_h[i, 0, 1, :, :]).then_inc(sxl[i % 3], 16)
                    sync.dma_start(out=r1b[:, i % 3, DH:D_MODEL],
                                   in_=a2o_h[i, 1, 0, :, :]).then_inc(sxl[i % 3], 16)
                    sync.dma_start(out=abuf[:, 1, :],
                                   in_=a2o_h[i, 1, 1, :, :]).then_inc(sxl[i % 3], 16)
                    if dbg in (True, 1):
                        sync.wait_ge(sxl[i % 3], 64 * (i // 3 + 1))
                        if i >= 1:
                            sync.wait_ge(sdbga, 16 * i)
                        sync.dma_start(out=dbg_abuf_h[i],
                                       in_=abuf[:]).then_inc(sdbga, 16)
                if dbg in (True, 2):
                    idb = tile_A(s - 1)
                    if idb is not None:
                        sync.wait_ge(svar, idb + 1)
                        if idb >= 1:
                            sync.wait_ge(sdbgr, 16 * idb)
                        sync.dma_start(out=dbg_r1b_h[idb],
                                       in_=r1b[:, idb % 3]).then_inc(sdbgr, 16)
                if dbg == 3 and tile_A(s + 1) is not None:
                    sync.wait_ge(sxl[tile_A(s + 1) % 3], 64 * (tile_A(s + 1) // 3 + 1))
                if dbg == 4:
                    idb = tile_A(s - 1)
                    if idb is not None:
                        sync.wait_ge(svar, idb + 1)
            sync.wait_ge(sowp, 16 * NT)
            if dbg == 7:
                for rr in range(3):
                    sync.wait_ge(sdbgr, 16 * rr)
                    sync.dma_start(out=dbg_r1b_h[rr],
                                   in_=r1b[:, rr]).then_inc(sdbgr, 16)
                sync.wait_ge(sdbgr, 48)
                sync.dma_start(out=dbg_abuf_h[0],
                               in_=abuf[:]).then_inc(sdbgr, 16)
                sync.wait_ge(sdbgr, 64)

        @block.gpsimd
        def _(gpsimd):
            gpsimd.memset(cn05[:], -0.5)
            if dbg == 5:
                for i in range(NT):
                    gpsimd.wait_ge(svar, i + 1)
                    if i >= 1:
                        gpsimd.wait_ge(sdbgr, 16 * i)
                    gpsimd.dma_start(out=dbg_r1b_h[i],
                                     in_=r1b[:, i % 3]).then_inc(sdbgr, 16)
            for s in range(NSLOT):
                k = tile_E(s)
                if k is not None:
                    gpsimd.wait_ge(svar2, k + 1)
                    gpsimd.tensor_tensor(stat[:, k % 4, 10:11], stat[:, k % 4, 9:10],
                                         cn05[:, 0:1], ALU.pow).then_inc(srstd2, 1)
                i = tile_A(s)
                if i is not None:
                    gpsimd.wait_ge(svar, i + 1)
                    gpsimd.tensor_tensor(stat[:, i % 4, 2:3], stat[:, i % 4, 1:2],
                                         cn05[:, 0:1], ALU.pow).then_inc(srstd, 1)
                    if dbg == 5:
                        if i >= 1:
                            gpsimd.wait_ge(sdbgr, 16 * i)
                        gpsimd.dma_start(out=dbg_r1b_h[i],
                                         in_=r1b[:, i % 3]).then_inc(sdbgr, 16)
                if 1 <= s <= NCH:
                    cc = s - 1
                    gpsimd.wait_ge(srwp[cc % 2], 16 * (cc // 2 + 1))
                    gpsimd.collective_compute(
                        "AllGather", ALU.bypass,
                        ins=[a2i_h[cc]],
                        outs=[a2o_h[cc]],
                        replica_groups=[[0, 1], [2, 3], [4, 5], [6, 7]],
                    ).then_inc(sa2a, 1)

        @block.tensor
        def _(tensor):
            tensor.wait_ge(sld, 128)
            tensor.wait_ge(swst, NWT + 1)
            u = ("stats",)
            tensor.matmul(ps_m1[0:2, m1_par[u], 0:1], acc2[:, 0:2], onescol[:, 0:1],
                          start=True, stop=True).then_inc(smms, 1)

            def f16tr(src_ap, key):
                k = f16_idx[key]
                if k % 8 == 0 and k >= 16:
                    tensor.wait_ge(seh, (k - 16) // 8 + 1)
                tensor.transpose(ps_th[:, (k // 8) % 2, k % 8], src_ap,
                                 identh[:]).then_inc(sth, 1)

            def f32tr(src_ap, key):
                k = f32_idx[key]
                if k % 4 == 0 and k >= 8:
                    tensor.wait_ge(sef, (k - 8) // 4 + 1)
                tensor.transpose(ps_tf[:, (k // 4) % 2, k % 4], src_ap,
                                 identf[:]).then_inc(stf, 1)

            def m1_take(u):
                if u in m1_prev:
                    tensor.wait_ge(*m1_prev[u])
                return m1_par[u]

            for j in range(NWT):
                tensor.wait_ge(swq, j + 1)
                for k in range(8 if j < NW1 else 32):
                    f16tr(wqh[:, k * 128:(k + 1) * 128], ("w", j, k))
            for s in range(NSLOT):
                c = s
                if c < NCH:
                    for tt in range(2):
                        g = 2 * c + tt
                        tensor.wait_ge(sxp[g % 2], 16 * (g // 2 + 1))
                        for d8 in range(8):
                            f32tr(xbuf[:, g % 2, d8 * 128:(d8 + 1) * 128],
                                  ("x", c, tt, d8))
                    tensor.wait_ge(sef, f32_idx[("x", c, 1, 7)] // 4 + 1)
                    par = m1_take(("pre", c))
                    for d8 in range(8):
                        mm = tensor.matmul(ps_m1[0:96, par, 0:TC], wcat[:, d8],
                                           xT[:, c % 2, d8],
                                           start=(d8 == 0), stop=(d8 == 7))
                        if d8 == 7:
                            mm.then_inc(smm1, 1)
                    tensor.wait_ge(spre, c + 1)
                    for dt in range(4):
                        par = m1_take(("dlt", c, dt))
                        tensor.matmul(ps_m1[:, par, 0:TC], wdt[:, dt],
                                      preSB[0:64, c % 2],
                                      start=True, stop=True).then_inc(smmd, 1)
                i = tile_A(s)
                if i is not None:
                    tensor.wait_ge(sxq, i + 1)
                    for k in range(8):
                        f16tr(xqh[:, k * 128:(k + 1) * 128], ("xq", i, k))
                if 1 <= s <= NCH:
                    cc = s - 1
                    tensor.wait_ge(sr1v, cc + 1)
                    for tt in range(2):
                        for dt in range(4):
                            f32tr(r1T[:, dt, tt * 128:(tt + 1) * 128],
                                  ("r", cc, tt, dt))
                if i is not None:
                    tensor.wait_ge(seh, f16_idx[("xq", i, 7)] // 8 + 1)
                    for hc in range(8):
                        par = m1_take(("ffn", i, hc))
                        for dt in range(8):
                            mm = tensor.matmul(ps_m1[:, par, :], xqT[:, dt],
                                               w1T[:, dt, hc * 512:(hc + 1) * 512],
                                               start=(dt == 0), stop=(dt == 7))
                            if dt == 7:
                                mm.then_inc(sm1, 1)
                if i is not None:
                    tensor.wait_ge(shq, i + 1)
                    for k in range(32):
                        f16tr(hqh[:, k * 128:(k + 1) * 128], ("hq", i, k))
                    tensor.wait_ge(seh, f16_idx[("hq", i, 31)] // 8 + 1)
                    for e2 in range(2):
                        if i >= 1:
                            tensor.wait_ge(sr2, i)
                        for ht in range(32):
                            mm = tensor.matmul(ps_m2[:, e2, :], hqT[:, ht],
                                               w2T[:, ht, e2 * 512:(e2 + 1) * 512],
                                               start=(ht == 0), stop=(ht == 31))
                            if ht == 31:
                                mm.then_inc(sm2, 1)

        @block.scalar
        def _(scalar):
            scalar.wait_ge(sld, 128)
            scalar.wait_ge(sprep, 1)
            scalar.activation(aneg[:], aneg[:], AF.Exp).then_inc(sprep, 1)
            scalar.wait_ge(sprep, 3)
            nsfx = 0

            def f16evac_group(k0):
                scalar.wait_ge(sth, k0 + 8)
                bank = (k0 // 8) % 2
                for q in range(8):
                    uq = f16_seq[k0 + q]
                    src = ps_th[:, bank, q]
                    if uq[0] == "w":
                        j, kk = uq[1], uq[2]
                        dst = (w1T[:, kk, j * 128:(j + 1) * 128] if j < NW1
                               else w2T[:, kk, (j - NW1) * 128:(j - NW1 + 1) * 128])
                    elif uq[0] == "xq":
                        dst = xqT[:, uq[2], :]
                    else:
                        dst = hqT[:, uq[2], :]
                    op = scalar.copy(dst, src)
                op.then_inc(seh, 1)

            def f32evac_group(k0):
                scalar.wait_ge(stf, k0 + 4)
                bank = (k0 // 4) % 2
                for q in range(4):
                    uq = f32_seq[k0 + q]
                    src = ps_tf[:, bank, q]
                    if uq[0] == "x":
                        cq, ttq, d8q = uq[1], uq[2], uq[3]
                        dst = xT[:, cq % 2, d8q, ttq * 128:(ttq + 1) * 128]
                    else:
                        ccq, ttq, dtq = uq[1], uq[2], uq[3]
                        dst = r1o[:, ttq, dtq * 128:(dtq + 1) * 128]
                    op = scalar.copy(dst, src)
                op.then_inc(sef, 1)

            nf16 = 0
            for j in range(NWT):
                for k in range(8 if j < NW1 else 32):
                    if nf16 % 8 == 7:
                        f16evac_group(nf16 - 7)
                    nf16 += 1
            nf32 = 0
            for s in range(NSLOT):
                c = s
                if c < NCH:
                    if c >= 2:
                        scalar.wait_ge(smm1, c - 1)
                        scalar.wait_ge(sr1v, c - 1)
                    for tt in range(2):
                        for d8 in range(8):
                            if nf32 % 4 == 3:
                                f32evac_group(nf32 - 3)
                            nf32 += 1
                    for dt in range(4):
                        idx = 4 * c + dt
                        scalar.wait_ge(smmd, idx + 1)
                        spb = xbuf[:, c % 2, dt * TC:(dt + 1) * TC]
                        psl = ps_m1[:, m1_par[("dlt", c, dt)], 0:TC]
                        scalar.activation(spb, psl, AF.Abs,
                                          bias=bdt[:, dt:dt + 1]).then_inc(ssfx, 1)
                        nsfx += 1
                        scalar.wait_ge(ssfx, nsfx)
                        scalar.activation(spb, spb, AF.Exp,
                                          scale=cm1[:, 0:1]).then_inc(ssfx, 1)
                        nsfx += 1
                        scalar.wait_ge(ssfx, nsfx)
                        scalar.activation(spb, spb, AF.Ln, bias=1.0)
                        scalar.activation(delta[:, c % 2, dt], psl, AF.Relu,
                                          bias=bdt[:, dt:dt + 1]).then_inc(ssp, 1)
                i = tile_A(s)
                if 1 <= s <= NCH:
                    ce = s - 1
                    for n in range(2, NST):
                        for dt in range(4):
                            scalar.wait_ge(ssc, 64 * ce + 4 * (n - 2) + dt + 1)
                            scalar.activation(e0[:, (ce + n) % 2, dt],
                                              delta[:, ce % 2, dt], AF.Exp,
                                              scale=aneg[:, dt, n:n + 1]).then_inc(se0, 1)
                if 1 <= s <= NCH:
                    cc = s - 1
                    if cc >= 1:
                        scalar.wait_ge(srwp[(cc - 1) % 2], 16 * ((cc - 1) // 2 + 1))
                    for tt in range(2):
                        for dt in range(4):
                            if nf32 % 4 == 3:
                                f32evac_group(nf32 - 3)
                            nf32 += 1
                if i is not None:
                    if i >= 1:
                        scalar.wait_ge(sm2, 2 * i)
                    for k in range(8):
                        if nf16 % 8 == 7:
                            f16evac_group(nf16 - 7)
                        nf16 += 1
                    scalar.wait_ge(sgsc, i + 1)
                    if i >= 1:
                        scalar.wait_ge(shq, i)
                    for hc in range(8):
                        scalar.wait_ge(sm1, 8 * i + hc + 1)
                        scalar.activation(gb[:, hc * 512:(hc + 1) * 512],
                                          ps_m1[:, m1_par[("ffn", i, hc)], :],
                                          AF.Gelu,
                                          scale=stat[:, i % 4, 5:6]).then_inc(sg, 1)
                if i is not None:
                    for k in range(32):
                        if nf16 % 8 == 7:
                            f16evac_group(nf16 - 7)
                        nf16 += 1
                if c < NCH:
                    for n in range(2):
                        for dt in range(4):
                            scalar.wait_ge(sdel, 4 * c + dt + 1)
                            if c >= 1:
                                npr = 15 if n == 0 else 14
                                scalar.wait_ge(ssc, 64 * (c - 1) + 4 * npr + dt + 1)
                            scalar.activation(e0[:, (c + n) % 2, dt],
                                              delta[:, c % 2, dt], AF.Exp,
                                              scale=aneg[:, dt, n:n + 1]).then_inc(se0, 1)

        @block.vector
        def _(vector):
            vector.memset(cm1[:], -1.0)
            vector.memset(onescol[:], 1.0)
            vector.memset(hstate[:], 0.0).then_inc(sprep, 1)
            vector.wait_ge(sld, 128)
            vector.tensor_copy(hmask[:, 0:1], hmaskf[:, 0:1])
            vector.wait_ge(sprep, 2)
            vector.tensor_scalar(aneg[:], aneg[:], -1.0, None, ALU.mult)
            vector.tensor_scalar(dplus[:], dplus[:], 1.0, None, ALU.add).then_inc(sprep, 1)
            for j in range(NWT):
                vector.wait_ge(swl, 16 * (j + 1))
                wsl = wload[:, 0:D_MODEL] if j < NW1 else wload[:, :]
                col = 0 if j < NW1 else 1
                if j == 0 or j == NW1:
                    vector.reduce_sum(acc2[:, col:col + 1], wsl, axis=AX.X,
                                      apply_absolute_value=True).then_inc(swst, 1)
                else:
                    vector.reduce_sum(sq[:, 0:1], wsl, axis=AX.X,
                                      apply_absolute_value=True)
                    vector.drain()
                    vector.tensor_add(acc2[:, col:col + 1], acc2[:, col:col + 1],
                                      sq[:, 0:1]).then_inc(swst, 1)
            vector.drain().then_inc(swst, 1)
            vector.wait_ge(smms, 1)
            vector.tensor_scalar(svsmall[:, 0:1],
                                 ps_m1[0:2, m1_par[("stats",)], 0:1],
                                 1.0 / (HID * D_MODEL), 1e-5, ALU.mult, ALU.max)
            vector.drain()
            vector.reciprocal(svsmall[:, 1:2], svsmall[:, 0:1])
            vector.drain().then_inc(sscal, 1)
            vector.wait_ge(sbs, 32)
            for j in range(NWT):
                vector.wait_ge(swl, 16 * (NWT + j + 1))
                col = 1 if j < NW1 else 3
                if j >= 1:
                    prev_trs = sum(8 if jj < NW1 else 32 for jj in range(j))
                    vector.wait_ge(sth, prev_trs)
                nchk = 1 if j < NW1 else 4
                for ch in range(nchk):
                    wsl = wload[:, ch * D_MODEL:(ch + 1) * D_MODEL]
                    qsl = wqh[:, ch * D_MODEL:(ch + 1) * D_MODEL]
                    vector.tensor_scalar(sq[:], wsl, sval[:, col:col + 1], None,
                                         ALU.mult)
                    vector.tensor_scalar(qsl, sq[:], 0.5, -1.0, ALU.is_gt, ALU.add)
                    vector.tensor_scalar(xqh[:], sq[:], -0.5, None, ALU.is_ge)
                    vector.tensor_add(qsl, qsl, xqh[:])
                vector.drain().then_inc(swq, 1)
            for s in range(NSLOT):
                c = s
                j = tile_C(s)
                if j is not None:
                    st = stat[:, j % 4]
                    vector.wait_ge(sg, 8 * (j + 1))
                    vector.reduce_max(st[:, 6:7], gb[:], axis=AX.X,
                                      apply_absolute_value=True)
                    vector.drain()
                    vector.tensor_scalar(st[:, 6:7], st[:, 6:7], 1e-5, None, ALU.max)
                    vector.drain()
                    vector.reciprocal(st[:, 7:8], st[:, 6:7])
                    vector.drain()
                    vector.tensor_scalar(st[:, 7:8], st[:, 7:8], 127.0, None,
                                         ALU.mult)
                    vector.drain()
                    vector.tensor_scalar(gb[:], gb[:], st[:, 7:8], 127.0,
                                         ALU.mult, ALU.min)
                    vector.tensor_scalar(gb[:], gb[:], -128.0, 1536.0, ALU.max,
                                         ALU.add)
                    if j >= 1:
                        vector.wait_ge(sth, f16_idx[("hq", j - 1, 31)] + 1)
                    else:
                        vector.wait_ge(sth, NW1 * 8 + NW2 * 32)
                    vector.tensor_copy(hqh[:], gb[:])
                    vector.tensor_scalar(hqh[:], hqh[:], -1536.0, None, ALU.add)
                    vector.drain().then_inc(shq, 1)
                    vector.tensor_mul(st[:, 8:9], sval[:, 2:3], st[:, 6:7])
                    vector.drain()
                    vector.tensor_scalar(st[:, 8:9], st[:, 8:9], 1.0 / 127.0, None,
                                         ALU.mult)
                    vector.drain()
                k = tile_E(s)
                if k is not None:
                    st = stat[:, k % 4]
                    vector.wait_ge(sm2, 2 * (k + 1))
                    if k >= 1:
                        vector.wait_ge(sowp, 16 * k)
                    vector.tensor_scalar(sq[:], r1b[:, k % 3], st[:, 0:1],
                                         st[:, 2:3], ALU.subtract, ALU.mult)
                    for e2 in range(2):
                        vector.scalar_tensor_tensor(
                            outb[:, e2 * 512:(e2 + 1) * 512],
                            ps_m2[:, e2, :], st[:, 8:9],
                            sq[:, e2 * 512:(e2 + 1) * 512], ALU.mult, ALU.add)
                    vector.reduce_sum(st[:, 9:10], outb[:], axis=AX.X)
                    vector.drain()
                    vector.tensor_scalar(st[:, 9:10], st[:, 9:10], 1.0 / D_MODEL,
                                         None, ALU.mult)
                    vector.drain()
                    vector.tensor_scalar(outb[:], outb[:],
                                         st[:, 9:10], None, ALU.subtract)
                    vector.tensor_mul(sq[:], outb[:], outb[:])
                    vector.reduce_sum(st[:, 9:10], sq[:], axis=AX.X)
                    vector.drain()
                    vector.tensor_scalar(st[:, 9:10], st[:, 9:10], 1.0 / D_MODEL,
                                         EPS, ALU.mult, ALU.add).then_inc(svar2, 1)
                    vector.wait_ge(srstd2, k + 1)
                    vector.tensor_scalar(outb[:], outb[:],
                                         st[:, 10:11], None,
                                         ALU.mult).then_inc(sr2, 1)
                i = tile_A(s)
                if i is not None:
                    st = stat[:, i % 4]
                    vector.wait_ge(sxl[i % 3], 64 * (i // 3 + 1))
                    vector.copy_predicated(r1b[:, i % 3, 0:DH],
                                           hmask[:, 0:1].broadcast_to([128, DH]),
                                           abuf[:, 0, :])
                    vector.copy_predicated(r1b[:, i % 3, DH:D_MODEL],
                                           hmask[:, 0:1].broadcast_to([128, DH]),
                                           abuf[:, 1, :])
                    vector.drain()
                    vector.reduce_sum(st[:, 0:1], r1b[:, i % 3], axis=AX.X)
                    vector.drain()
                    vector.tensor_scalar(st[:, 0:1], st[:, 0:1], 1.0 / D_MODEL, None,
                                         ALU.mult)
                    vector.drain()
                    vector.tensor_scalar(cent[:], r1b[:, i % 3], st[:, 0:1], None,
                                         ALU.subtract)
                    vector.tensor_mul(sq[:], cent[:], cent[:])
                    vector.reduce_sum(st[:, 1:2], sq[:], axis=AX.X)
                    vector.drain()
                    vector.tensor_scalar(st[:, 1:2], st[:, 1:2], 1.0 / D_MODEL, EPS,
                                         ALU.mult, ALU.add).then_inc(svar, 1)
                    vector.reduce_max(st[:, 3:4], cent[:], axis=AX.X,
                                      apply_absolute_value=True)
                    vector.drain()
                    vector.tensor_scalar(st[:, 3:4], st[:, 3:4], 1e-5, None, ALU.max)
                    vector.drain()
                    vector.reciprocal(st[:, 4:5], st[:, 3:4])
                    vector.drain()
                    vector.tensor_scalar(st[:, 4:5], st[:, 4:5], 127.0, None, ALU.mult)
                    vector.drain()
                    vector.tensor_scalar(sq[:], cent[:], st[:, 4:5], 127.0,
                                         ALU.mult, ALU.min)
                    vector.tensor_scalar(sq[:], sq[:], -128.0, 1536.0, ALU.max,
                                         ALU.add)
                    if i >= 1:
                        vector.wait_ge(sth, f16_idx[("xq", i - 1, 7)] + 1)
                    vector.tensor_copy(xqh[:], sq[:])
                    vector.tensor_scalar(xqh[:], xqh[:], -1536.0, None,
                                         ALU.add)
                    vector.drain().then_inc(sxq, 1)
                    vector.wait_ge(srstd, i + 1)
                    vector.tensor_mul(st[:, 5:6], sval[:, 0:1], st[:, 3:4])
                    vector.drain()
                    vector.tensor_mul(st[:, 5:6], st[:, 5:6], st[:, 2:3])
                    vector.drain()
                    vector.tensor_scalar(st[:, 5:6], st[:, 5:6], 1.0 / 127.0, None,
                                         ALU.mult)
                    vector.drain().then_inc(sgsc, 1)
                if c < NCH:
                    vector.wait_ge(smm1, c + 1)
                    if c >= 2:
                        vector.wait_ge(smmd, 4 * (c - 1))
                        vector.wait_ge(sbw, 16 * (c - 1))
                    vector.tensor_copy(preSB[0:96, c % 2],
                                       ps_m1[0:96, m1_par[("pre", c)], 0:TC]).then_inc(spre, 1)
                if 1 <= s <= NCH:
                    cc = s - 1
                    for dt in range(4):
                        vector.tensor_mul(uu[:, dt], delta[:, cc % 2, dt],
                                          xT[:, cc % 2, dt])
                    for n in range(NST):
                        vector.wait_ge(sbrp[n % 2], 32 * (8 * cc + n // 2 + 1))
                        for dt in range(4):
                            mm = vector.tensor_mul(dbx[:, dt], uu[:, dt],
                                                   brep[:, n % 2])
                            if dt == 3:
                                mm.then_inc(sdbx, 1)
                        for dt in range(4):
                            vector.wait_ge(se0, 64 * cc + 4 * n + dt + 1)
                            vector.tensor_tensor_scan(
                                hbuf[:, dt], e0[:, (cc + n) % 2, dt], dbx[:, dt],
                                hstate[:, dt, n:n + 1], ALU.mult, ALU.add,
                            ).then_inc(ssc, 1)
                        for dt in range(4):
                            vector.tensor_copy(hstate[:, dt, n:n + 1],
                                               hbuf[:, dt, TC - 1:TC])
                        if n == 0:
                            for dt in range(4):
                                mm = vector.tensor_mul(yacc[:, dt], hbuf[:, dt],
                                                       crep[:, n % 2])
                                if dt == 3:
                                    mm.then_inc(scml, 1)
                        else:
                            vector.tensor_mul(tmpy[:, 0], hbuf[:, 0], crep[:, n % 2])
                            vector.tensor_mul(tmpy[:, 1], hbuf[:, 1], crep[:, n % 2])
                            vector.tensor_add(yacc[:, 0], yacc[:, 0], tmpy[:, 0])
                            vector.tensor_mul(tmpy[:, 0], hbuf[:, 2], crep[:, n % 2])
                            vector.tensor_add(yacc[:, 1], yacc[:, 1], tmpy[:, 1])
                            vector.tensor_mul(tmpy[:, 1], hbuf[:, 3], crep[:, n % 2])
                            vector.tensor_add(yacc[:, 2], yacc[:, 2], tmpy[:, 0])
                            vector.tensor_add(yacc[:, 3], yacc[:, 3],
                                              tmpy[:, 1]).then_inc(scml, 1)
                    for dt in range(4):
                        if cc >= 1:
                            vector.wait_ge(stf, f32_idx[("r", cc - 1, 1, 3)] + 1)
                        vector.scalar_tensor_tensor(
                            r1T[:, dt], xT[:, cc % 2, dt], dplus[:, dt:dt + 1],
                            yacc[:, dt], ALU.mult, ALU.add)
                    vector.drain().then_inc(sr1v, 1)
                if c < NCH:
                    for dt in range(4):
                        vector.wait_ge(ssp, 4 * c + dt + 1)
                        if c >= 2:
                            vector.wait_ge(se0, 64 * (c - 1))
                        vector.tensor_add(delta[:, c % 2, dt], delta[:, c % 2, dt],
                                          xbuf[:, c % 2, dt * TC:(dt + 1) * TC]
                                          ).then_inc(sdel, 1)
    return nc


def kernel(L0_attn_x, L0_attn_A_log, L0_attn_W_delta, L0_attn_W_dt, L0_attn_b_dt,
           L0_attn_W_B, L0_attn_W_C, L0_attn_D, L0_ffn_w1, L0_ffn_w2):
    x = np.ascontiguousarray(np.asarray(L0_attn_x, dtype=np.float32))
    alog = np.asarray(L0_attn_A_log, dtype=np.float32)
    wdelta = np.asarray(L0_attn_W_delta, dtype=np.float32)
    wdt = np.asarray(L0_attn_W_dt, dtype=np.float32)
    bdt = np.asarray(L0_attn_b_dt, dtype=np.float32)
    wb = np.asarray(L0_attn_W_B, dtype=np.float32)
    wc = np.asarray(L0_attn_W_C, dtype=np.float32)
    dvec = np.asarray(L0_attn_D, dtype=np.float32)
    w1 = np.ascontiguousarray(np.asarray(L0_ffn_w1, dtype=np.float32))
    w2 = np.ascontiguousarray(np.asarray(L0_ffn_w2, dtype=np.float32))

    wcat = np.ascontiguousarray(np.concatenate([wdelta, wb, wc], axis=1))
    identf = np.eye(128, dtype=np.float32)
    identh = np.eye(128, dtype=np.float16)

    nc = build_fused()
    in_maps = []
    for c in range(8):
        b, h = divmod(c, 2)
        d0 = h * DH
        if h == 0:
            xp = np.ascontiguousarray(x[b])
            wcp = wcat
        else:
            xp = np.ascontiguousarray(np.concatenate([x[b][:, DH:], x[b][:, :DH]], axis=1))
            wcp = np.ascontiguousarray(np.concatenate([wcat[DH:], wcat[:DH]], axis=0))
        in_maps.append({
            "x": xp,
            "hmask_d": np.full((DH,), float(h), np.float32),
            "wcat_d": wcp,
            "wdt_d": np.ascontiguousarray(wdt[:, d0:d0 + DH]),
            "bdt_d": np.ascontiguousarray(bdt[d0:d0 + DH]),
            "alog_d": np.ascontiguousarray(alog[d0:d0 + DH]),
            "dvec_d": np.ascontiguousarray(dvec[d0:d0 + DH]),
            "identf_d": identf,
            "identh_d": identh,
            "w1": w1,
            "w2": w2,
        })
    res = run_bass_kernel_spmd(nc, in_maps, core_ids=list(range(8)))

    out = np.empty((BATCH, SEQ, D_MODEL), dtype=np.float32)
    for c in range(8):
        b, h = divmod(c, 2)
        o = res.results[c]["out"]
        for i in range(NT):
            t0 = i * TC + h * 128
            out[b][t0:t0 + 128] = o[i * 128:(i + 1) * 128]
    return out


# version 10
# speedup vs baseline: 1.0293x; 1.0293x over previous
import contextlib
import numpy as np
import concourse.bass as bass
import concourse.mybir as mybir
from concourse.bass_utils import run_bass_kernel_spmd

F32 = mybir.dt.float32
F16 = mybir.dt.float16
F8 = mybir.dt.float8e4
U8 = mybir.dt.uint8
AF = mybir.ActivationFunctionType
ALU = mybir.AluOpType
AX = mybir.AxisListType

D_MODEL = 1024
DH = 512
SEQ = 8192
BATCH = 4
NST = 16
DT_RANK = 64
HID = 4096
TC = 256
NCH = SEQ // TC
NT = NCH
NW1 = HID // 128
NW2 = D_MODEL // 128
NWT = NW1 + NW2
EPS = 1e-5
LAG = 3
NSLOT = NCH + LAG + 3


def build_fused(dbg=False):
    nc = bass.Bass("TRN2", target_bir_lowering=False, detect_race_conditions=False)
    x_h = nc.dram_tensor("x", [SEQ, D_MODEL], F32, kind="ExternalInput").ap()
    wcat_h = nc.dram_tensor("wcat_d", [D_MODEL, 96], F32, kind="ExternalInput").ap()
    wdt_h = nc.dram_tensor("wdt_d", [DT_RANK, DH], F32, kind="ExternalInput").ap()
    bdt_h = nc.dram_tensor("bdt_d", [DH], F32, kind="ExternalInput").ap()
    alog_h = nc.dram_tensor("alog_d", [DH, NST], F32, kind="ExternalInput").ap()
    dvec_h = nc.dram_tensor("dvec_d", [DH], F32, kind="ExternalInput").ap()
    identf_h = nc.dram_tensor("identf_d", [128, 128], F32, kind="ExternalInput").ap()
    identh_h = nc.dram_tensor("identh_d", [128, 128], F16, kind="ExternalInput").ap()
    hmask_h = nc.dram_tensor("hmask_d", [DH], F32, kind="ExternalInput").ap()
    w1_h = nc.dram_tensor("w1", [HID, D_MODEL], F32, kind="ExternalInput").ap()
    w2_h = nc.dram_tensor("w2", [D_MODEL, HID], F32, kind="ExternalInput").ap()
    out_h = nc.dram_tensor("out", [NT * 128, D_MODEL], F32, kind="ExternalOutput").ap()
    bc_h = nc.dram_tensor("bc_scratch", [2, NST, NCH, TC], F32, kind="Internal").ap()
    a2i_h = nc.dram_tensor("a2a_in", [NCH, 2, 128, DH], F32, kind="Internal").ap()
    a2o_h = nc.dram_tensor("a2a_out", [NCH, 2, 2, 128, DH], F32, kind="Internal").ap()
    sc_h = nc.dram_tensor("sc_scratch", [4], F32, kind="Internal").ap()
    if dbg in (True, 1, 2, 5, 7):
        dbg_r1b_h = nc.dram_tensor("dbg_r1b", [NT, 128, D_MODEL], F32,
                                   kind="ExternalOutput").ap()
        dbg_abuf_h = nc.dram_tensor("dbg_abuf", [NT, 128, 2, DH], F32,
                                    kind="ExternalOutput").ap()
        sdbga = nc.semaphore("sdbga").__enter__()
        sdbgr = nc.semaphore("sdbgr").__enter__()

    es = contextlib.ExitStack()
    with es:
        sb = lambda n, s, d=F32: es.enter_context(nc.sbuf_tensor(n, s, d))
        psa = lambda n, s, d=F32: es.enter_context(nc.psum_tensor(n, s, d))
        sem = lambda n: es.enter_context(nc.semaphore(n))

        wcat = sb("wcat", [128, 8, 96])
        wdt = sb("wdt", [64, 4, 128])
        bdt = sb("bdt", [128, 4])
        aneg = sb("aneg", [128, 4, NST])
        dplus = sb("dplus", [128, 4])
        identf = sb("identf", [128, 128])
        identh = sb("identh", [128, 128], F16)
        cm1 = sb("cm1", [128, 1])
        cn05 = sb("cn05", [128, 1])
        onescol = sb("onescol", [128, 1])
        sval = sb("sval", [128, 4])
        svsmall = sb("svsmall", [2, 2])
        acc2 = sb("acc2", [128, 2])
        w1T = sb("w1T", [128, 8, HID], F8)
        w2T = sb("w2T", [128, 32, D_MODEL], F8)
        xbuf = sb("xbuf", [128, 2, D_MODEL])
        xT = sb("xT", [128, 2, 8, TC])
        preSB = sb("preSB", [128, 2, TC])
        delta = sb("delta", [128, 2, 4, TC])
        uu = sb("uu", [128, 4, TC])
        e0 = sb("e0", [128, 2, 4, TC])
        dbx = sb("dbx", [128, 4, TC])
        hbuf = sb("hbuf", [128, 4, TC])
        brep = sb("brep", [128, 2, TC])
        crep = sb("crep", [128, 2, TC])
        tmpy = sb("tmpy", [128, 2, TC])
        yacc = sb("yacc", [128, 4, TC])
        hstate = sb("hstate", [128, 4, NST])
        r1T = sb("r1T", [128, 4, TC])
        r1o = sb("r1o", [128, 2, DH])
        r1b = sb("r1b", [128, 3, D_MODEL])
        cent = sb("cent", [128, D_MODEL])
        sq = sb("sq", [128, D_MODEL])
        gb = sb("gb", [128, HID])
        hqh = sb("hqh", [128, HID], F16)
        hqT = sb("hqT", [128, 32, 128], F16)
        xqh = sb("xqh", [128, D_MODEL], F16)
        abuf = sb("abuf", [128, 2, DH])
        hmaskf = sb("hmaskf", [128, 4])
        hmask = sb("hmask", [128, 1], U8)
        outb = sb("outb", [128, D_MODEL])
        stat = sb("stat", [128, 4, 12])
        wload = gb
        wqh = hqh
        xqT = hqT[:, 24:32]

        ps_m1 = psa("ps_m1", [128, 2, 512])
        ps_m2 = psa("ps_m2", [128, 2, 512])
        ps_tf = psa("ps_tf", [128, 2, 4, 128])
        ps_th = psa("ps_th", [128, 2, 8, 128], F16)

        sld = sem("sld")
        sprep = sem("sprep")
        ssfx = sem("ssfx")
        swl = sem("swl")
        swst = sem("swst")
        smms = sem("smms")
        sscal = sem("sscal")
        sbs = sem("sbs")
        swq = sem("swq")
        sth = sem("sth")
        seh = sem("seh")
        sxp = [sem("sx0"), sem("sx1")]
        stf = sem("stf")
        sef = sem("sef")
        smm1 = sem("smm1")
        spre = sem("spre")
        smmd = sem("smmd")
        ssp = sem("ssp")
        sdel = sem("sdel")
        se0 = sem("se0")
        ssc = sem("ssc")
        sdbx = sem("sdbx")
        scml = sem("scml")
        sbrp = [sem("sbr0"), sem("sbr1")]
        sbw = sem("sbw")
        sr1v = sem("sr1v")
        srwp = [sem("srw0"), sem("srw1")]
        sa2a = sem("sa2a")
        sxl = [sem("sxl0"), sem("sxl1"), sem("sxl2")]
        svar = sem("svar")
        srstd = sem("srstd")
        sxq = sem("sxq")
        sgsc = sem("sgsc")
        sm1 = sem("sm1")
        sg = sem("sg")
        shq = sem("shq")
        sm2 = sem("sm2")
        svar2 = sem("svar2")
        srstd2 = sem("srstd2")
        sr2 = sem("sr2")
        sowp = sem("sow")

        block = es.enter_context(nc.Block())

        def tile_A(s):
            i = s - LAG
            return i if 0 <= i < NT else None

        def tile_C(s):
            i = s - LAG - 1
            return i if 0 <= i < NT else None

        def tile_E(s):
            i = s - LAG - 2
            return i if 0 <= i < NT else None

        m1_seq = [("stats",)]
        for s in range(NSLOT):
            if s < NCH:
                m1_seq.append(("pre", s))
                for dt in range(4):
                    m1_seq.append(("dlt", s, dt))
            i = tile_A(s)
            if i is not None:
                for hc in range(8):
                    m1_seq.append(("ffn", i, hc))
        m1_par = {u: k % 2 for k, u in enumerate(m1_seq)}

        def m1_consumer(u):
            if u[0] == "stats":
                return (sscal, 1)
            if u[0] == "pre":
                return (spre, u[1] + 1)
            if u[0] == "dlt":
                return (ssp, 4 * u[1] + u[2] + 1)
            return (sg, 8 * u[1] + u[2] + 1)

        m1_prev = {}
        for k, u in enumerate(m1_seq):
            if k >= 2:
                m1_prev[u] = m1_consumer(m1_seq[k - 2])

        f32_seq = []
        for s in range(NSLOT):
            if s < NCH:
                for tt in range(2):
                    for d8 in range(8):
                        f32_seq.append(("x", s, tt, d8))
            if 1 <= s <= NCH:
                for tt in range(2):
                    for dt in range(4):
                        f32_seq.append(("r", s - 1, tt, dt))
        f32_idx = {u: k for k, u in enumerate(f32_seq)}
        assert len(f32_seq) % 4 == 0

        f16_seq = []
        for j in range(NWT):
            for k in range(8 if j < NW1 else 32):
                f16_seq.append(("w", j, k))
        for i in range(NT):
            for k in range(8):
                f16_seq.append(("xq", i, k))
            for k in range(32):
                f16_seq.append(("hq", i, k))
        f16_idx = {u: k for k, u in enumerate(f16_seq)}
        assert len(f16_seq) % 8 == 0

        @block.sync
        def _(sync):
            sync.dma_start(out=wcat[:], in_=wcat_h.rearrange("(a p) c -> p a c", p=128)).then_inc(sld, 16)
            sync.dma_start(out=wdt[:], in_=wdt_h.rearrange("r (a p) -> r a p", p=128)).then_inc(sld, 16)
            sync.dma_start(out=bdt[:], in_=bdt_h.rearrange("(a p) -> p a", p=128),
                           allow_slow_non_contiguous=True).then_inc(sld, 16)
            sync.dma_start(out=aneg[:], in_=alog_h.rearrange("(a p) n -> p a n", p=128)).then_inc(sld, 16)
            sync.dma_start(out=dplus[:], in_=dvec_h.rearrange("(a p) -> p a", p=128),
                           allow_slow_non_contiguous=True).then_inc(sld, 16)
            sync.dma_start(out=identf[:], in_=identf_h).then_inc(sld, 16)
            sync.dma_start(out=identh[:], in_=identh_h).then_inc(sld, 16)
            sync.dma_start(out=hmaskf[:], in_=hmask_h.rearrange("(a p) -> p a", p=128),
                           allow_slow_non_contiguous=True).then_inc(sld, 16)
            for j in range(NWT):
                if j >= 1:
                    sync.wait_ge(swst, j)
                src = (w1_h[j * 128:(j + 1) * 128, :] if j < NW1
                       else w2_h[(j - NW1) * 128:(j - NW1 + 1) * 128, :])
                sync.dma_start(out=(wload[:, 0:D_MODEL] if j < NW1 else wload[:, :]),
                               in_=src).then_inc(swl, 16)
            sync.wait_ge(sscal, 1)
            sync.dma_start(out=sc_h.rearrange("(a b) -> a b", a=2),
                           in_=svsmall[:, :]).then_inc(sbs, 16)
            sync.wait_ge(sbs, 16)
            sync.dma_start(out=sval[:, :],
                           in_=sc_h.unsqueeze(0).broadcast_to([128, 4])).then_inc(sbs, 16)
            sync.wait_ge(swst, NWT)
            for j in range(NWT):
                if j >= 1:
                    sync.wait_ge(swq, j)
                src = (w1_h[j * 128:(j + 1) * 128, :] if j < NW1
                       else w2_h[(j - NW1) * 128:(j - NW1 + 1) * 128, :])
                sync.dma_start(out=(wload[:, 0:D_MODEL] if j < NW1 else wload[:, :]),
                               in_=src).then_inc(swl, 16)
            for s in range(NSLOT):
                c = s
                if c < NCH:
                    for tt in range(2):
                        g = 2 * c + tt
                        if g >= 2:
                            sync.wait_ge(stf, f32_idx[("x", (g - 2) // 2, (g - 2) % 2, 7)] + 1)
                        if tt == (c - 1) % 2:
                            if c >= 1:
                                sync.wait_ge(sdel, 4 * c)
                        elif c >= 2:
                            sync.wait_ge(sdel, 4 * (c - 1))
                        sync.dma_start(
                            out=xbuf[:, g % 2],
                            in_=x_h[g * 128:(g + 1) * 128, :],
                        ).then_inc(sxp[g % 2], 16)
                k = tile_E(s - 1)
                if k is not None:
                    sync.wait_ge(sr2, k + 1)
                    sync.dma_start(out=out_h[k * 128:(k + 1) * 128, :],
                                   in_=outb[:]).then_inc(sowp, 16)
                if 1 <= s <= NCH:
                    cc = s - 1
                    sync.wait_ge(sbw, 16 * (cc + 1))
                    for n in range(NST):
                        gi = NST * cc + n
                        if gi >= 2:
                            pc, pn = divmod(gi - 2, NST)
                            sync.wait_ge(sdbx, 16 * pc + pn + 1)
                            sync.wait_ge(scml, 16 * pc + pn + 1)
                        sync.dma_start(
                            out=brep[:, n % 2],
                            in_=bc_h[0, n, cc, :].unsqueeze(0).broadcast_to([128, TC]),
                        ).then_inc(sbrp[n % 2], 16)
                        sync.dma_start(
                            out=crep[:, n % 2],
                            in_=bc_h[1, n, cc, :].unsqueeze(0).broadcast_to([128, TC]),
                        ).then_inc(sbrp[n % 2], 16)
                if c < NCH:
                    sync.wait_ge(spre, c + 1)
                    sync.dma_start(out=bc_h[:, :, c, :],
                                   in_=preSB[64:96, c % 2]).then_inc(sbw, 16)
                if 1 <= s <= NCH:
                    sync.wait_ge(sef, f32_idx[("r", cc, 1, 3)] // 4 + 1)
                    sync.dma_start(out=a2i_h[cc].rearrange("b t d -> t b d"),
                                   in_=r1o[:, :, :]).then_inc(srwp[cc % 2], 16)
                i = tile_A(s + 1)
                if i is not None:
                    sync.wait_ge(sa2a, i + 1)
                    if i >= 3:
                        sync.wait_ge(svar2, i - 2)
                    if i >= 1:
                        sync.wait_ge(svar, i)
                    sync.dma_start(out=r1b[:, i % 3, 0:DH],
                                   in_=a2o_h[i, 0, 0, :, :]).then_inc(sxl[i % 3], 16)
                    sync.dma_start(out=abuf[:, 0, :],
                                   in_=a2o_h[i, 0, 1, :, :]).then_inc(sxl[i % 3], 16)
                    sync.dma_start(out=r1b[:, i % 3, DH:D_MODEL],
                                   in_=a2o_h[i, 1, 0, :, :]).then_inc(sxl[i % 3], 16)
                    sync.dma_start(out=abuf[:, 1, :],
                                   in_=a2o_h[i, 1, 1, :, :]).then_inc(sxl[i % 3], 16)
                    if dbg in (True, 1):
                        sync.wait_ge(sxl[i % 3], 64 * (i // 3 + 1))
                        if i >= 1:
                            sync.wait_ge(sdbga, 16 * i)
                        sync.dma_start(out=dbg_abuf_h[i],
                                       in_=abuf[:]).then_inc(sdbga, 16)
                if dbg in (True, 2):
                    idb = tile_A(s - 1)
                    if idb is not None:
                        sync.wait_ge(svar, idb + 1)
                        if idb >= 1:
                            sync.wait_ge(sdbgr, 16 * idb)
                        sync.dma_start(out=dbg_r1b_h[idb],
                                       in_=r1b[:, idb % 3]).then_inc(sdbgr, 16)
                if dbg == 3 and tile_A(s + 1) is not None:
                    sync.wait_ge(sxl[tile_A(s + 1) % 3], 64 * (tile_A(s + 1) // 3 + 1))
                if dbg == 4:
                    idb = tile_A(s - 1)
                    if idb is not None:
                        sync.wait_ge(svar, idb + 1)
            sync.wait_ge(sowp, 16 * NT)
            if dbg == 7:
                for rr in range(3):
                    sync.wait_ge(sdbgr, 16 * rr)
                    sync.dma_start(out=dbg_r1b_h[rr],
                                   in_=r1b[:, rr]).then_inc(sdbgr, 16)
                sync.wait_ge(sdbgr, 48)
                sync.dma_start(out=dbg_abuf_h[0],
                               in_=abuf[:]).then_inc(sdbgr, 16)
                sync.wait_ge(sdbgr, 64)

        @block.gpsimd
        def _(gpsimd):
            gpsimd.memset(cn05[:], -0.5)
            if dbg == 5:
                for i in range(NT):
                    gpsimd.wait_ge(svar, i + 1)
                    if i >= 1:
                        gpsimd.wait_ge(sdbgr, 16 * i)
                    gpsimd.dma_start(out=dbg_r1b_h[i],
                                     in_=r1b[:, i % 3]).then_inc(sdbgr, 16)
            for s in range(NSLOT):
                k = tile_E(s)
                if k is not None:
                    gpsimd.wait_ge(svar2, k + 1)
                    gpsimd.tensor_tensor(stat[:, k % 4, 10:11], stat[:, k % 4, 9:10],
                                         cn05[:, 0:1], ALU.pow).then_inc(srstd2, 1)
                i = tile_A(s)
                if i is not None:
                    gpsimd.wait_ge(svar, i + 1)
                    gpsimd.tensor_tensor(stat[:, i % 4, 2:3], stat[:, i % 4, 1:2],
                                         cn05[:, 0:1], ALU.pow).then_inc(srstd, 1)
                    if dbg == 5:
                        if i >= 1:
                            gpsimd.wait_ge(sdbgr, 16 * i)
                        gpsimd.dma_start(out=dbg_r1b_h[i],
                                         in_=r1b[:, i % 3]).then_inc(sdbgr, 16)
                if 1 <= s <= NCH:
                    cc = s - 1
                    gpsimd.wait_ge(srwp[cc % 2], 16 * (cc // 2 + 1))
                    gpsimd.collective_compute(
                        "AllGather", ALU.bypass,
                        ins=[a2i_h[cc]],
                        outs=[a2o_h[cc]],
                        replica_groups=[[0, 1], [2, 3], [4, 5], [6, 7]],
                    ).then_inc(sa2a, 1)

        @block.tensor
        def _(tensor):
            tensor.wait_ge(sld, 128)
            tensor.wait_ge(swst, NWT + 1)
            u = ("stats",)
            tensor.matmul(ps_m1[0:2, m1_par[u], 0:1], acc2[:, 0:2], onescol[:, 0:1],
                          start=True, stop=True).then_inc(smms, 1)

            def f16tr(src_ap, key):
                k = f16_idx[key]
                if k % 8 == 0 and k >= 16:
                    tensor.wait_ge(seh, (k - 16) // 8 + 1)
                tensor.transpose(ps_th[:, (k // 8) % 2, k % 8], src_ap,
                                 identh[:]).then_inc(sth, 1)

            def f32tr(src_ap, key):
                k = f32_idx[key]
                if k % 4 == 0 and k >= 8:
                    tensor.wait_ge(sef, (k - 8) // 4 + 1)
                tensor.transpose(ps_tf[:, (k // 4) % 2, k % 4], src_ap,
                                 identf[:]).then_inc(stf, 1)

            def m1_take(u):
                if u in m1_prev:
                    tensor.wait_ge(*m1_prev[u])
                return m1_par[u]

            for j in range(NWT):
                tensor.wait_ge(swq, j + 1)
                for k in range(8 if j < NW1 else 32):
                    f16tr(wqh[:, k * 128:(k + 1) * 128], ("w", j, k))
            for s in range(NSLOT):
                c = s
                if c < NCH:
                    for tt in range(2):
                        g = 2 * c + tt
                        tensor.wait_ge(sxp[g % 2], 16 * (g // 2 + 1))
                        for d8 in range(8):
                            f32tr(xbuf[:, g % 2, d8 * 128:(d8 + 1) * 128],
                                  ("x", c, tt, d8))
                    tensor.wait_ge(sef, f32_idx[("x", c, 1, 7)] // 4 + 1)
                    par = m1_take(("pre", c))
                    for d8 in range(8):
                        mm = tensor.matmul(ps_m1[0:96, par, 0:TC], wcat[:, d8],
                                           xT[:, c % 2, d8],
                                           start=(d8 == 0), stop=(d8 == 7))
                        if d8 == 7:
                            mm.then_inc(smm1, 1)
                    tensor.wait_ge(spre, c + 1)
                    for dt in range(4):
                        par = m1_take(("dlt", c, dt))
                        tensor.matmul(ps_m1[:, par, 0:TC], wdt[:, dt],
                                      preSB[0:64, c % 2],
                                      start=True, stop=True).then_inc(smmd, 1)
                if 1 <= s <= NCH:
                    cc = s - 1
                    tensor.wait_ge(sr1v, cc + 1)
                    for tt in range(2):
                        for dt in range(4):
                            f32tr(r1T[:, dt, tt * 128:(tt + 1) * 128],
                                  ("r", cc, tt, dt))
                i = tile_A(s)
                if i is not None:
                    tensor.wait_ge(sxq, i + 1)
                    for k in range(8):
                        f16tr(xqh[:, k * 128:(k + 1) * 128], ("xq", i, k))
                    tensor.wait_ge(seh, f16_idx[("xq", i, 7)] // 8 + 1)
                    for hc in range(8):
                        par = m1_take(("ffn", i, hc))
                        for dt in range(8):
                            mm = tensor.matmul(ps_m1[:, par, :], xqT[:, dt],
                                               w1T[:, dt, hc * 512:(hc + 1) * 512],
                                               start=(dt == 0), stop=(dt == 7))
                            if dt == 7:
                                mm.then_inc(sm1, 1)
                if i is not None:
                    tensor.wait_ge(shq, i + 1)
                    for k in range(32):
                        f16tr(hqh[:, k * 128:(k + 1) * 128], ("hq", i, k))
                    tensor.wait_ge(seh, f16_idx[("hq", i, 31)] // 8 + 1)
                    for e2 in range(2):
                        if i >= 1:
                            tensor.wait_ge(sr2, i)
                        for ht in range(32):
                            mm = tensor.matmul(ps_m2[:, e2, :], hqT[:, ht],
                                               w2T[:, ht, e2 * 512:(e2 + 1) * 512],
                                               start=(ht == 0), stop=(ht == 31))
                            if ht == 31:
                                mm.then_inc(sm2, 1)

        @block.scalar
        def _(scalar):
            scalar.wait_ge(sld, 128)
            scalar.wait_ge(sprep, 1)
            scalar.activation(aneg[:], aneg[:], AF.Exp).then_inc(sprep, 1)
            scalar.wait_ge(sprep, 3)
            nsfx = 0

            def f16evac_group(k0):
                scalar.wait_ge(sth, k0 + 8)
                bank = (k0 // 8) % 2
                for q in range(8):
                    uq = f16_seq[k0 + q]
                    src = ps_th[:, bank, q]
                    if uq[0] == "w":
                        j, kk = uq[1], uq[2]
                        dst = (w1T[:, kk, j * 128:(j + 1) * 128] if j < NW1
                               else w2T[:, kk, (j - NW1) * 128:(j - NW1 + 1) * 128])
                    elif uq[0] == "xq":
                        dst = xqT[:, uq[2], :]
                    else:
                        dst = hqT[:, uq[2], :]
                    op = scalar.copy(dst, src)
                op.then_inc(seh, 1)

            def f32evac_group(k0):
                scalar.wait_ge(stf, k0 + 4)
                bank = (k0 // 4) % 2
                for q in range(4):
                    uq = f32_seq[k0 + q]
                    src = ps_tf[:, bank, q]
                    if uq[0] == "x":
                        cq, ttq, d8q = uq[1], uq[2], uq[3]
                        dst = xT[:, cq % 2, d8q, ttq * 128:(ttq + 1) * 128]
                    else:
                        ccq, ttq, dtq = uq[1], uq[2], uq[3]
                        dst = r1o[:, ttq, dtq * 128:(dtq + 1) * 128]
                    op = scalar.copy(dst, src)
                op.then_inc(sef, 1)

            nf16 = 0
            for j in range(NWT):
                for k in range(8 if j < NW1 else 32):
                    if nf16 % 8 == 7:
                        f16evac_group(nf16 - 7)
                    nf16 += 1
            nf32 = 0
            for s in range(NSLOT):
                c = s
                if c < NCH:
                    if c >= 2:
                        scalar.wait_ge(smm1, c - 1)
                        scalar.wait_ge(sr1v, c - 1)
                    for tt in range(2):
                        for d8 in range(8):
                            if nf32 % 4 == 3:
                                f32evac_group(nf32 - 3)
                            nf32 += 1
                    for dt in range(4):
                        idx = 4 * c + dt
                        scalar.wait_ge(smmd, idx + 1)
                        spb = xbuf[:, c % 2, dt * TC:(dt + 1) * TC]
                        psl = ps_m1[:, m1_par[("dlt", c, dt)], 0:TC]
                        scalar.activation(spb, psl, AF.Abs,
                                          bias=bdt[:, dt:dt + 1]).then_inc(ssfx, 1)
                        nsfx += 1
                        scalar.wait_ge(ssfx, nsfx)
                        scalar.activation(spb, spb, AF.Exp,
                                          scale=cm1[:, 0:1]).then_inc(ssfx, 1)
                        nsfx += 1
                        scalar.wait_ge(ssfx, nsfx)
                        scalar.activation(spb, spb, AF.Ln, bias=1.0)
                        scalar.activation(delta[:, c % 2, dt], psl, AF.Relu,
                                          bias=bdt[:, dt:dt + 1]).then_inc(ssp, 1)
                i = tile_A(s)
                if 1 <= s <= NCH:
                    ce = s - 1
                    for n in range(2, NST):
                        for dt in range(4):
                            scalar.wait_ge(ssc, 64 * ce + 4 * (n - 2) + dt + 1)
                            scalar.activation(e0[:, (ce + n) % 2, dt],
                                              delta[:, ce % 2, dt], AF.Exp,
                                              scale=aneg[:, dt, n:n + 1]).then_inc(se0, 1)
                if 1 <= s <= NCH:
                    cc = s - 1
                    if cc >= 1:
                        scalar.wait_ge(srwp[(cc - 1) % 2], 16 * ((cc - 1) // 2 + 1))
                    for tt in range(2):
                        for dt in range(4):
                            if nf32 % 4 == 3:
                                f32evac_group(nf32 - 3)
                            nf32 += 1
                if i is not None:
                    if i >= 1:
                        scalar.wait_ge(sm2, 2 * i)
                    for k in range(8):
                        if nf16 % 8 == 7:
                            f16evac_group(nf16 - 7)
                        nf16 += 1
                    scalar.wait_ge(sgsc, i + 1)
                    if i >= 1:
                        scalar.wait_ge(shq, i)
                    for hc in range(8):
                        scalar.wait_ge(sm1, 8 * i + hc + 1)
                        scalar.activation(gb[:, hc * 512:(hc + 1) * 512],
                                          ps_m1[:, m1_par[("ffn", i, hc)], :],
                                          AF.Gelu,
                                          scale=stat[:, i % 4, 5:6]).then_inc(sg, 1)
                if i is not None:
                    for k in range(32):
                        if nf16 % 8 == 7:
                            f16evac_group(nf16 - 7)
                        nf16 += 1
                if c < NCH:
                    for n in range(2):
                        for dt in range(4):
                            scalar.wait_ge(sdel, 4 * c + dt + 1)
                            if c >= 1:
                                npr = 15 if n == 0 else 14
                                scalar.wait_ge(ssc, 64 * (c - 1) + 4 * npr + dt + 1)
                            scalar.activation(e0[:, (c + n) % 2, dt],
                                              delta[:, c % 2, dt], AF.Exp,
                                              scale=aneg[:, dt, n:n + 1]).then_inc(se0, 1)

        @block.vector
        def _(vector):
            vector.memset(cm1[:], -1.0)
            vector.memset(onescol[:], 1.0)
            vector.memset(hstate[:], 0.0).then_inc(sprep, 1)
            vector.wait_ge(sld, 128)
            vector.tensor_copy(hmask[:, 0:1], hmaskf[:, 0:1])
            vector.wait_ge(sprep, 2)
            vector.tensor_scalar(aneg[:], aneg[:], -1.0, None, ALU.mult)
            vector.tensor_scalar(dplus[:], dplus[:], 1.0, None, ALU.add).then_inc(sprep, 1)
            for j in range(NWT):
                vector.wait_ge(swl, 16 * (j + 1))
                wsl = wload[:, 0:D_MODEL] if j < NW1 else wload[:, :]
                col = 0 if j < NW1 else 1
                if j == 0 or j == NW1:
                    vector.reduce_sum(acc2[:, col:col + 1], wsl, axis=AX.X,
                                      apply_absolute_value=True).then_inc(swst, 1)
                else:
                    vector.reduce_sum(sq[:, 0:1], wsl, axis=AX.X,
                                      apply_absolute_value=True)
                    vector.drain()
                    vector.tensor_add(acc2[:, col:col + 1], acc2[:, col:col + 1],
                                      sq[:, 0:1]).then_inc(swst, 1)
            vector.drain().then_inc(swst, 1)
            vector.wait_ge(smms, 1)
            vector.tensor_scalar(svsmall[:, 0:1],
                                 ps_m1[0:2, m1_par[("stats",)], 0:1],
                                 1.0 / (HID * D_MODEL), 1e-5, ALU.mult, ALU.max)
            vector.drain()
            vector.reciprocal(svsmall[:, 1:2], svsmall[:, 0:1])
            vector.drain().then_inc(sscal, 1)
            vector.wait_ge(sbs, 32)
            for j in range(NWT):
                vector.wait_ge(swl, 16 * (NWT + j + 1))
                col = 1 if j < NW1 else 3
                if j >= 1:
                    prev_trs = sum(8 if jj < NW1 else 32 for jj in range(j))
                    vector.wait_ge(sth, prev_trs)
                nchk = 1 if j < NW1 else 4
                for ch in range(nchk):
                    wsl = wload[:, ch * D_MODEL:(ch + 1) * D_MODEL]
                    qsl = wqh[:, ch * D_MODEL:(ch + 1) * D_MODEL]
                    vector.tensor_scalar(sq[:], wsl, sval[:, col:col + 1], None,
                                         ALU.mult)
                    vector.tensor_scalar(qsl, sq[:], 0.5, -1.0, ALU.is_gt, ALU.add)
                    vector.tensor_scalar(xqh[:], sq[:], -0.5, None, ALU.is_ge)
                    vector.tensor_add(qsl, qsl, xqh[:])
                vector.drain().then_inc(swq, 1)
            for s in range(NSLOT):
                c = s
                j = tile_C(s)
                if j is not None:
                    st = stat[:, j % 4]
                    vector.wait_ge(srstd, j + 1)
                    vector.tensor_mul(st[:, 5:6], sval[:, 0:1], st[:, 3:4])
                    vector.drain()
                    vector.tensor_mul(st[:, 5:6], st[:, 5:6], st[:, 2:3])
                    vector.drain()
                    vector.tensor_scalar(st[:, 5:6], st[:, 5:6], 1.0 / 127.0, None,
                                         ALU.mult)
                    vector.drain().then_inc(sgsc, 1)
                    vector.wait_ge(sg, 8 * (j + 1))
                    vector.reduce_max(st[:, 6:7], gb[:], axis=AX.X,
                                      apply_absolute_value=True)
                    vector.drain()
                    vector.tensor_scalar(st[:, 6:7], st[:, 6:7], 1e-5, None, ALU.max)
                    vector.drain()
                    vector.reciprocal(st[:, 7:8], st[:, 6:7])
                    vector.drain()
                    vector.tensor_scalar(st[:, 7:8], st[:, 7:8], 127.0, None,
                                         ALU.mult)
                    vector.drain()
                    vector.tensor_scalar(gb[:], gb[:], st[:, 7:8], 1536.0,
                                         ALU.mult, ALU.add)
                    if j >= 1:
                        vector.wait_ge(sth, f16_idx[("hq", j - 1, 31)] + 1)
                    else:
                        vector.wait_ge(sth, NW1 * 8 + NW2 * 32)
                    vector.tensor_copy(hqh[:], gb[:])
                    vector.tensor_scalar(hqh[:], hqh[:], -1536.0, None, ALU.add)
                    vector.drain().then_inc(shq, 1)
                    vector.tensor_mul(st[:, 8:9], sval[:, 2:3], st[:, 6:7])
                    vector.drain()
                    vector.tensor_scalar(st[:, 8:9], st[:, 8:9], 1.0 / 127.0, None,
                                         ALU.mult)
                    vector.drain()
                k = tile_E(s)
                if k is not None:
                    st = stat[:, k % 4]
                    vector.wait_ge(sm2, 2 * (k + 1))
                    if k >= 1:
                        vector.wait_ge(sowp, 16 * k)
                    vector.tensor_scalar(sq[:], r1b[:, k % 3], st[:, 0:1],
                                         st[:, 2:3], ALU.subtract, ALU.mult)
                    for e2 in range(2):
                        vector.scalar_tensor_tensor(
                            outb[:, e2 * 512:(e2 + 1) * 512],
                            ps_m2[:, e2, :], st[:, 8:9],
                            sq[:, e2 * 512:(e2 + 1) * 512], ALU.mult, ALU.add)
                    vector.reduce_sum(st[:, 9:10], outb[:], axis=AX.X)
                    vector.drain()
                    vector.tensor_scalar(st[:, 9:10], st[:, 9:10], 1.0 / D_MODEL,
                                         None, ALU.mult)
                    vector.drain()
                    vector.tensor_scalar(outb[:], outb[:],
                                         st[:, 9:10], None, ALU.subtract)
                    vector.tensor_mul(sq[:], outb[:], outb[:])
                    vector.reduce_sum(st[:, 9:10], sq[:], axis=AX.X)
                    vector.drain()
                    vector.tensor_scalar(st[:, 9:10], st[:, 9:10], 1.0 / D_MODEL,
                                         EPS, ALU.mult, ALU.add).then_inc(svar2, 1)
                    vector.wait_ge(srstd2, k + 1)
                    vector.tensor_scalar(outb[:], outb[:],
                                         st[:, 10:11], None,
                                         ALU.mult).then_inc(sr2, 1)
                i = tile_A(s)
                if i is not None:
                    st = stat[:, i % 4]
                    vector.wait_ge(sxl[i % 3], 64 * (i // 3 + 1))
                    vector.copy_predicated(r1b[:, i % 3, 0:DH],
                                           hmask[:, 0:1].broadcast_to([128, DH]),
                                           abuf[:, 0, :])
                    vector.copy_predicated(r1b[:, i % 3, DH:D_MODEL],
                                           hmask[:, 0:1].broadcast_to([128, DH]),
                                           abuf[:, 1, :])
                    vector.drain()
                    vector.reduce_sum(st[:, 0:1], r1b[:, i % 3], axis=AX.X)
                    vector.drain()
                    vector.tensor_scalar(st[:, 0:1], st[:, 0:1], 1.0 / D_MODEL, None,
                                         ALU.mult)
                    vector.drain()
                    vector.tensor_scalar(cent[:], r1b[:, i % 3], st[:, 0:1], None,
                                         ALU.subtract)
                    vector.tensor_mul(sq[:], cent[:], cent[:])
                    vector.reduce_sum(st[:, 1:2], sq[:], axis=AX.X)
                    vector.drain()
                    vector.tensor_scalar(st[:, 1:2], st[:, 1:2], 1.0 / D_MODEL, EPS,
                                         ALU.mult, ALU.add).then_inc(svar, 1)
                    vector.reduce_max(st[:, 3:4], cent[:], axis=AX.X,
                                      apply_absolute_value=True)
                    vector.drain()
                    vector.tensor_scalar(st[:, 3:4], st[:, 3:4], 1e-5, None, ALU.max)
                    vector.drain()
                    vector.reciprocal(st[:, 4:5], st[:, 3:4])
                    vector.drain()
                    vector.tensor_scalar(st[:, 4:5], st[:, 4:5], 127.0, None, ALU.mult)
                    vector.drain()
                    vector.tensor_scalar(sq[:], cent[:], st[:, 4:5], 1536.0,
                                         ALU.mult, ALU.add)
                    if i >= 1:
                        vector.wait_ge(sth, f16_idx[("xq", i - 1, 7)] + 1)
                    vector.tensor_copy(xqh[:], sq[:])
                    vector.tensor_scalar(xqh[:], xqh[:], -1536.0, None,
                                         ALU.add)
                    vector.drain().then_inc(sxq, 1)
                if c < NCH:
                    vector.wait_ge(smm1, c + 1)
                    if c >= 2:
                        vector.wait_ge(smmd, 4 * (c - 1))
                        vector.wait_ge(sbw, 16 * (c - 1))
                    vector.tensor_copy(preSB[0:96, c % 2],
                                       ps_m1[0:96, m1_par[("pre", c)], 0:TC]).then_inc(spre, 1)
                if 1 <= s <= NCH:
                    cc = s - 1
                    for dt in range(4):
                        vector.tensor_mul(uu[:, dt], delta[:, cc % 2, dt],
                                          xT[:, cc % 2, dt])
                    for n in range(NST):
                        vector.wait_ge(sbrp[n % 2], 32 * (8 * cc + n // 2 + 1))
                        vector.tensor_mul(
                            dbx[:, :], uu[:, :],
                            brep[:, n % 2:n % 2 + 1, :].broadcast_to([128, 4, TC])
                        ).then_inc(sdbx, 1)
                        for dt in range(4):
                            vector.wait_ge(se0, 64 * cc + 4 * n + dt + 1)
                            vector.tensor_tensor_scan(
                                hbuf[:, dt], e0[:, (cc + n) % 2, dt], dbx[:, dt],
                                hstate[:, dt, n:n + 1], ALU.mult, ALU.add,
                            ).then_inc(ssc, 1)
                        for dt in range(4):
                            vector.tensor_copy(hstate[:, dt, n:n + 1],
                                               hbuf[:, dt, TC - 1:TC])
                        if n == 0:
                            for dt in range(4):
                                mm = vector.tensor_mul(yacc[:, dt], hbuf[:, dt],
                                                       crep[:, n % 2])
                                if dt == 3:
                                    mm.then_inc(scml, 1)
                        else:
                            vector.tensor_mul(tmpy[:, 0], hbuf[:, 0], crep[:, n % 2])
                            vector.tensor_mul(tmpy[:, 1], hbuf[:, 1], crep[:, n % 2])
                            vector.tensor_add(yacc[:, 0], yacc[:, 0], tmpy[:, 0])
                            vector.tensor_mul(tmpy[:, 0], hbuf[:, 2], crep[:, n % 2])
                            vector.tensor_add(yacc[:, 1], yacc[:, 1], tmpy[:, 1])
                            vector.tensor_mul(tmpy[:, 1], hbuf[:, 3], crep[:, n % 2])
                            vector.tensor_add(yacc[:, 2], yacc[:, 2], tmpy[:, 0])
                            vector.tensor_add(yacc[:, 3], yacc[:, 3],
                                              tmpy[:, 1]).then_inc(scml, 1)
                    for dt in range(4):
                        if cc >= 1:
                            vector.wait_ge(stf, f32_idx[("r", cc - 1, 1, 3)] + 1)
                        vector.scalar_tensor_tensor(
                            r1T[:, dt], xT[:, cc % 2, dt], dplus[:, dt:dt + 1],
                            yacc[:, dt], ALU.mult, ALU.add)
                    vector.drain().then_inc(sr1v, 1)
                if c < NCH:
                    for dt in range(4):
                        vector.wait_ge(ssp, 4 * c + dt + 1)
                        if c >= 2:
                            vector.wait_ge(se0, 64 * (c - 1))
                        vector.tensor_add(delta[:, c % 2, dt], delta[:, c % 2, dt],
                                          xbuf[:, c % 2, dt * TC:(dt + 1) * TC]
                                          ).then_inc(sdel, 1)
    return nc


def kernel(L0_attn_x, L0_attn_A_log, L0_attn_W_delta, L0_attn_W_dt, L0_attn_b_dt,
           L0_attn_W_B, L0_attn_W_C, L0_attn_D, L0_ffn_w1, L0_ffn_w2):
    x = np.ascontiguousarray(np.asarray(L0_attn_x, dtype=np.float32))
    alog = np.asarray(L0_attn_A_log, dtype=np.float32)
    wdelta = np.asarray(L0_attn_W_delta, dtype=np.float32)
    wdt = np.asarray(L0_attn_W_dt, dtype=np.float32)
    bdt = np.asarray(L0_attn_b_dt, dtype=np.float32)
    wb = np.asarray(L0_attn_W_B, dtype=np.float32)
    wc = np.asarray(L0_attn_W_C, dtype=np.float32)
    dvec = np.asarray(L0_attn_D, dtype=np.float32)
    w1 = np.ascontiguousarray(np.asarray(L0_ffn_w1, dtype=np.float32))
    w2 = np.ascontiguousarray(np.asarray(L0_ffn_w2, dtype=np.float32))

    wcat = np.ascontiguousarray(np.concatenate([wdelta, wb, wc], axis=1))
    identf = np.eye(128, dtype=np.float32)
    identh = np.eye(128, dtype=np.float16)

    nc = build_fused()
    in_maps = []
    for c in range(8):
        b, h = divmod(c, 2)
        d0 = h * DH
        if h == 0:
            xp = np.ascontiguousarray(x[b])
            wcp = wcat
        else:
            xp = np.ascontiguousarray(np.concatenate([x[b][:, DH:], x[b][:, :DH]], axis=1))
            wcp = np.ascontiguousarray(np.concatenate([wcat[DH:], wcat[:DH]], axis=0))
        in_maps.append({
            "x": xp,
            "hmask_d": np.full((DH,), float(h), np.float32),
            "wcat_d": wcp,
            "wdt_d": np.ascontiguousarray(wdt[:, d0:d0 + DH]),
            "bdt_d": np.ascontiguousarray(bdt[d0:d0 + DH]),
            "alog_d": np.ascontiguousarray(alog[d0:d0 + DH]),
            "dvec_d": np.ascontiguousarray(dvec[d0:d0 + DH]),
            "identf_d": identf,
            "identh_d": identh,
            "w1": w1,
            "w2": w2,
        })
    res = run_bass_kernel_spmd(nc, in_maps, core_ids=list(range(8)))

    out = np.empty((BATCH, SEQ, D_MODEL), dtype=np.float32)
    for c in range(8):
        b, h = divmod(c, 2)
        o = res.results[c]["out"]
        for i in range(NT):
            t0 = i * TC + h * 128
            out[b][t0:t0 + 128] = o[i * 128:(i + 1) * 128]
    return out


# version 11
# speedup vs baseline: 1.0532x; 1.0232x over previous
import contextlib
import numpy as np
import concourse.bass as bass
import concourse.mybir as mybir
from concourse.bass_utils import run_bass_kernel_spmd

F32 = mybir.dt.float32
F16 = mybir.dt.float16
F8 = mybir.dt.float8e4
U8 = mybir.dt.uint8
AF = mybir.ActivationFunctionType
ALU = mybir.AluOpType
AX = mybir.AxisListType

D_MODEL = 1024
DH = 512
SEQ = 8192
BATCH = 4
NST = 16
DT_RANK = 64
HID = 4096
TC = 256
NCH = SEQ // TC
NT = NCH
NW1 = HID // 128
NW2 = D_MODEL // 128
NWT = NW1 + NW2
EPS = 1e-5
LAG = 3
NSLOT = NCH + LAG + 3


def build_fused(dbg=False):
    nc = bass.Bass("TRN2", target_bir_lowering=False, detect_race_conditions=False)
    x_h = nc.dram_tensor("x", [SEQ, D_MODEL], F32, kind="ExternalInput").ap()
    wcat_h = nc.dram_tensor("wcat_d", [D_MODEL, 96], F32, kind="ExternalInput").ap()
    wdt_h = nc.dram_tensor("wdt_d", [DT_RANK, DH], F32, kind="ExternalInput").ap()
    bdt_h = nc.dram_tensor("bdt_d", [DH], F32, kind="ExternalInput").ap()
    alog_h = nc.dram_tensor("alog_d", [DH, NST], F32, kind="ExternalInput").ap()
    dvec_h = nc.dram_tensor("dvec_d", [DH], F32, kind="ExternalInput").ap()
    identf_h = nc.dram_tensor("identf_d", [128, 128], F32, kind="ExternalInput").ap()
    identh_h = nc.dram_tensor("identh_d", [128, 128], F16, kind="ExternalInput").ap()
    hmask_h = nc.dram_tensor("hmask_d", [DH], F32, kind="ExternalInput").ap()
    w1_h = nc.dram_tensor("w1", [HID, D_MODEL], F32, kind="ExternalInput").ap()
    w2_h = nc.dram_tensor("w2", [D_MODEL, HID], F32, kind="ExternalInput").ap()
    out_h = nc.dram_tensor("out", [NT * 128, D_MODEL], F32, kind="ExternalOutput").ap()
    bc_h = nc.dram_tensor("bc_scratch", [2, NST, NCH, TC], F32, kind="Internal").ap()
    a2i_h = nc.dram_tensor("a2a_in", [NCH, 2, 128, DH], F32, kind="Internal").ap()
    a2o_h = nc.dram_tensor("a2a_out", [NCH, 2, 2, 128, DH], F32, kind="Internal").ap()
    sc_h = nc.dram_tensor("sc_scratch", [4], F32, kind="Internal").ap()
    if dbg in (True, 1, 2, 5, 7):
        dbg_r1b_h = nc.dram_tensor("dbg_r1b", [NT, 128, D_MODEL], F32,
                                   kind="ExternalOutput").ap()
        dbg_abuf_h = nc.dram_tensor("dbg_abuf", [NT, 128, 2, DH], F32,
                                    kind="ExternalOutput").ap()
        sdbga = nc.semaphore("sdbga").__enter__()
        sdbgr = nc.semaphore("sdbgr").__enter__()

    es = contextlib.ExitStack()
    with es:
        sb = lambda n, s, d=F32: es.enter_context(nc.sbuf_tensor(n, s, d))
        psa = lambda n, s, d=F32: es.enter_context(nc.psum_tensor(n, s, d))
        sem = lambda n: es.enter_context(nc.semaphore(n))

        wcat = sb("wcat", [128, 8, 96])
        wdt = sb("wdt", [64, 4, 128])
        bdt = sb("bdt", [128, 4])
        aneg = sb("aneg", [128, 4, NST])
        dplus = sb("dplus", [128, 4])
        identf = sb("identf", [128, 128])
        identh = sb("identh", [128, 128], F16)
        cm1 = sb("cm1", [128, 1])
        cn05 = sb("cn05", [128, 1])
        onescol = sb("onescol", [128, 1])
        sval = sb("sval", [128, 4])
        svsmall = sb("svsmall", [2, 2])
        acc2 = sb("acc2", [128, 2])
        w1T = sb("w1T", [128, 8, HID], F8)
        w2T = sb("w2T", [128, 32, D_MODEL], F8)
        xbuf = sb("xbuf", [128, 2, D_MODEL])
        xT = sb("xT", [128, 2, 8, TC])
        preSB = sb("preSB", [128, 2, TC])
        delta = sb("delta", [128, 2, 4, TC])
        uu = sb("uu", [128, 4, TC])
        e0 = sb("e0", [128, 2, 4, TC])
        dbx = sb("dbx", [128, 4, TC])
        hbuf = sb("hbuf", [128, 4, TC])
        brep = sb("brep", [128, 2, TC])
        crep = sb("crep", [128, 2, TC])
        tmpy = sb("tmpy", [128, 2, TC])
        yacc = sb("yacc", [128, 4, TC])
        hstate = sb("hstate", [128, 4, NST])
        r1T = sb("r1T", [128, 4, TC])
        r1o = sb("r1o", [128, 2, DH])
        r1b = sb("r1b", [128, 3, D_MODEL])
        cent = sb("cent", [128, D_MODEL])
        sq = sb("sq", [128, D_MODEL])
        gb = sb("gb", [128, HID])
        hqh = sb("hqh", [128, HID], F16)
        hqT = sb("hqT", [128, 32, 128], F16)
        xqh = sb("xqh", [128, D_MODEL], F16)
        abuf = sb("abuf", [128, 2, DH])
        hmaskf = sb("hmaskf", [128, 4])
        hmask = sb("hmask", [128, 1], U8)
        outb = sb("outb", [128, D_MODEL])
        stat = sb("stat", [128, 4, 12])
        wload = gb
        wqh = hqh
        xqT = hqT[:, 24:32]

        ps_m1 = psa("ps_m1", [128, 2, 512])
        ps_m2 = psa("ps_m2", [128, 2, 512])
        ps_tf = psa("ps_tf", [128, 2, 4, 128])
        ps_th = psa("ps_th", [128, 2, 8, 128], F16)

        sld = sem("sld")
        sprep = sem("sprep")
        ssfx = sem("ssfx")
        swl = sem("swl")
        swst = sem("swst")
        smms = sem("smms")
        sscal = sem("sscal")
        sbs = sem("sbs")
        swq = sem("swq")
        sth = sem("sth")
        seh = sem("seh")
        sxp = [sem("sx0"), sem("sx1")]
        stf = sem("stf")
        sef = sem("sef")
        smm1 = sem("smm1")
        spre = sem("spre")
        smmd = sem("smmd")
        ssp = sem("ssp")
        sdel = sem("sdel")
        se0 = sem("se0")
        ssc = sem("ssc")
        sdbx = sem("sdbx")
        scml = sem("scml")
        sbrp = [sem("sbr0"), sem("sbr1")]
        sbw = sem("sbw")
        sr1v = sem("sr1v")
        srwp = [sem("srw0"), sem("srw1")]
        sa2a = sem("sa2a")
        sxl = [sem("sxl0"), sem("sxl1"), sem("sxl2")]
        svar = sem("svar")
        srstd = sem("srstd")
        sxq = sem("sxq")
        sgsc = sem("sgsc")
        sm1 = sem("sm1")
        sg = sem("sg")
        shq = sem("shq")
        sm2 = sem("sm2")
        svar2 = sem("svar2")
        srstd2 = sem("srstd2")
        sr2 = sem("sr2")
        sowp = sem("sow")

        block = es.enter_context(nc.Block())

        def tile_A(s):
            i = s - LAG
            return i if 0 <= i < NT else None

        def tile_C(s):
            i = s - LAG - 1
            return i if 0 <= i < NT else None

        def tile_E(s):
            i = s - LAG - 2
            return i if 0 <= i < NT else None

        m1_seq = [("stats",)]
        for s in range(NSLOT):
            if s < NCH:
                m1_seq.append(("pre", s))
                for dt in range(4):
                    m1_seq.append(("dlt", s, dt))
            i = tile_A(s)
            if i is not None:
                for hc in range(8):
                    m1_seq.append(("ffn", i, hc))
        m1_par = {u: k % 2 for k, u in enumerate(m1_seq)}

        def m1_consumer(u):
            if u[0] == "stats":
                return (sscal, 1)
            if u[0] == "pre":
                return (spre, u[1] + 1)
            if u[0] == "dlt":
                return (ssp, 4 * u[1] + u[2] + 1)
            return (sg, 8 * u[1] + u[2] + 1)

        m1_prev = {}
        for k, u in enumerate(m1_seq):
            if k >= 2:
                m1_prev[u] = m1_consumer(m1_seq[k - 2])

        f32_seq = []
        for s in range(NSLOT):
            if s < NCH:
                for tt in range(2):
                    for d8 in range(8):
                        f32_seq.append(("x", s, tt, d8))
            if 1 <= s <= NCH:
                for tt in range(2):
                    for dt in range(4):
                        f32_seq.append(("r", s - 1, tt, dt))
        f32_idx = {u: k for k, u in enumerate(f32_seq)}
        assert len(f32_seq) % 4 == 0

        f16_seq = []
        for j in range(NWT):
            for k in range(8 if j < NW1 else 32):
                f16_seq.append(("w", j, k))
        for i in range(NT):
            for k in range(8):
                f16_seq.append(("xq", i, k))
            for k in range(32):
                f16_seq.append(("hq", i, k))
        f16_idx = {u: k for k, u in enumerate(f16_seq)}
        assert len(f16_seq) % 8 == 0

        @block.sync
        def _(sync):
            sync.dma_start(out=wcat[:], in_=wcat_h.rearrange("(a p) c -> p a c", p=128)).then_inc(sld, 16)
            sync.dma_start(out=wdt[:], in_=wdt_h.rearrange("r (a p) -> r a p", p=128)).then_inc(sld, 16)
            sync.dma_start(out=bdt[:], in_=bdt_h.rearrange("(a p) -> p a", p=128),
                           allow_slow_non_contiguous=True).then_inc(sld, 16)
            sync.dma_start(out=aneg[:], in_=alog_h.rearrange("(a p) n -> p a n", p=128)).then_inc(sld, 16)
            sync.dma_start(out=dplus[:], in_=dvec_h.rearrange("(a p) -> p a", p=128),
                           allow_slow_non_contiguous=True).then_inc(sld, 16)
            sync.dma_start(out=identf[:], in_=identf_h).then_inc(sld, 16)
            sync.dma_start(out=identh[:], in_=identh_h).then_inc(sld, 16)
            sync.dma_start(out=hmaskf[:], in_=hmask_h.rearrange("(a p) -> p a", p=128),
                           allow_slow_non_contiguous=True).then_inc(sld, 16)
            for j in range(NWT):
                if j >= 1:
                    sync.wait_ge(swst, j)
                src = (w1_h[j * 128:(j + 1) * 128, :] if j < NW1
                       else w2_h[(j - NW1) * 128:(j - NW1 + 1) * 128, :])
                sync.dma_start(out=(wload[:, 0:D_MODEL] if j < NW1 else wload[:, :]),
                               in_=src).then_inc(swl, 16)
            sync.wait_ge(sscal, 1)
            sync.dma_start(out=sc_h.rearrange("(a b) -> a b", a=2),
                           in_=svsmall[:, :]).then_inc(sbs, 16)
            sync.wait_ge(sbs, 16)
            sync.dma_start(out=sval[:, :],
                           in_=sc_h.unsqueeze(0).broadcast_to([128, 4])).then_inc(sbs, 16)
            sync.wait_ge(swst, NWT)
            for j in range(NWT):
                if j >= 1:
                    sync.wait_ge(swq, j)
                src = (w1_h[j * 128:(j + 1) * 128, :] if j < NW1
                       else w2_h[(j - NW1) * 128:(j - NW1 + 1) * 128, :])
                sync.dma_start(out=(wload[:, 0:D_MODEL] if j < NW1 else wload[:, :]),
                               in_=src).then_inc(swl, 16)
            for s in range(NSLOT):
                c = s
                if c < NCH:
                    for tt in range(2):
                        g = 2 * c + tt
                        if g >= 2:
                            sync.wait_ge(stf, f32_idx[("x", (g - 2) // 2, (g - 2) % 2, 7)] + 1)
                        if tt == (c - 1) % 2:
                            if c >= 1:
                                sync.wait_ge(sdel, 4 * c)
                        elif c >= 2:
                            sync.wait_ge(sdel, 4 * (c - 1))
                        sync.dma_start(
                            out=xbuf[:, g % 2],
                            in_=x_h[g * 128:(g + 1) * 128, :],
                        ).then_inc(sxp[g % 2], 16)
                k = tile_E(s - 1)
                if k is not None:
                    sync.wait_ge(sr2, k + 1)
                    sync.dma_start(out=out_h[k * 128:(k + 1) * 128, :],
                                   in_=outb[:]).then_inc(sowp, 16)
                if 1 <= s <= NCH:
                    cc = s - 1
                    sync.wait_ge(sbw, 16 * (cc + 1))
                    for n in range(NST):
                        gi = NST * cc + n
                        if gi >= 2:
                            pc, pn = divmod(gi - 2, NST)
                            sync.wait_ge(sdbx, 16 * pc + pn + 1)
                            sync.wait_ge(scml, 16 * pc + pn + 1)
                        sync.dma_start(
                            out=brep[:, n % 2],
                            in_=bc_h[0, n, cc, :].unsqueeze(0).broadcast_to([128, TC]),
                        ).then_inc(sbrp[n % 2], 16)
                        sync.dma_start(
                            out=crep[:, n % 2],
                            in_=bc_h[1, n, cc, :].unsqueeze(0).broadcast_to([128, TC]),
                        ).then_inc(sbrp[n % 2], 16)
                if c < NCH:
                    sync.wait_ge(spre, c + 1)
                    sync.dma_start(out=bc_h[:, :, c, :],
                                   in_=preSB[64:96, c % 2]).then_inc(sbw, 16)
                if 1 <= s <= NCH:
                    sync.wait_ge(sef, f32_idx[("r", cc, 1, 3)] // 4 + 1)
                    sync.dma_start(out=a2i_h[cc].rearrange("b t d -> t b d"),
                                   in_=r1o[:, :, :]).then_inc(srwp[cc % 2], 16)
                i = tile_A(s + 1)
                if i is not None:
                    sync.wait_ge(sa2a, i + 1)
                    if i >= 3:
                        sync.wait_ge(svar2, i - 2)
                    if i >= 1:
                        sync.wait_ge(svar, i)
                    sync.dma_start(out=r1b[:, i % 3, 0:DH],
                                   in_=a2o_h[i, 0, 0, :, :]).then_inc(sxl[i % 3], 16)
                    sync.dma_start(out=abuf[:, 0, :],
                                   in_=a2o_h[i, 0, 1, :, :]).then_inc(sxl[i % 3], 16)
                    sync.dma_start(out=r1b[:, i % 3, DH:D_MODEL],
                                   in_=a2o_h[i, 1, 0, :, :]).then_inc(sxl[i % 3], 16)
                    sync.dma_start(out=abuf[:, 1, :],
                                   in_=a2o_h[i, 1, 1, :, :]).then_inc(sxl[i % 3], 16)
                    if dbg in (True, 1):
                        sync.wait_ge(sxl[i % 3], 64 * (i // 3 + 1))
                        if i >= 1:
                            sync.wait_ge(sdbga, 16 * i)
                        sync.dma_start(out=dbg_abuf_h[i],
                                       in_=abuf[:]).then_inc(sdbga, 16)
                if dbg in (True, 2):
                    idb = tile_A(s - 1)
                    if idb is not None:
                        sync.wait_ge(svar, idb + 1)
                        if idb >= 1:
                            sync.wait_ge(sdbgr, 16 * idb)
                        sync.dma_start(out=dbg_r1b_h[idb],
                                       in_=r1b[:, idb % 3]).then_inc(sdbgr, 16)
                if dbg == 3 and tile_A(s + 1) is not None:
                    sync.wait_ge(sxl[tile_A(s + 1) % 3], 64 * (tile_A(s + 1) // 3 + 1))
                if dbg == 4:
                    idb = tile_A(s - 1)
                    if idb is not None:
                        sync.wait_ge(svar, idb + 1)
            sync.wait_ge(sowp, 16 * NT)
            if dbg == 7:
                for rr in range(3):
                    sync.wait_ge(sdbgr, 16 * rr)
                    sync.dma_start(out=dbg_r1b_h[rr],
                                   in_=r1b[:, rr]).then_inc(sdbgr, 16)
                sync.wait_ge(sdbgr, 48)
                sync.dma_start(out=dbg_abuf_h[0],
                               in_=abuf[:]).then_inc(sdbgr, 16)
                sync.wait_ge(sdbgr, 64)

        @block.gpsimd
        def _(gpsimd):
            gpsimd.memset(cn05[:], -0.5)
            if dbg == 5:
                for i in range(NT):
                    gpsimd.wait_ge(svar, i + 1)
                    if i >= 1:
                        gpsimd.wait_ge(sdbgr, 16 * i)
                    gpsimd.dma_start(out=dbg_r1b_h[i],
                                     in_=r1b[:, i % 3]).then_inc(sdbgr, 16)
            for s in range(NSLOT):
                k = tile_E(s)
                if k is not None:
                    gpsimd.wait_ge(svar2, k + 1)
                    gpsimd.tensor_tensor(stat[:, k % 4, 10:11], stat[:, k % 4, 9:10],
                                         cn05[:, 0:1], ALU.pow).then_inc(srstd2, 1)
                i = tile_A(s)
                if i is not None:
                    gpsimd.wait_ge(svar, i + 1)
                    gpsimd.tensor_tensor(stat[:, i % 4, 2:3], stat[:, i % 4, 1:2],
                                         cn05[:, 0:1], ALU.pow).then_inc(srstd, 1)
                    if dbg == 5:
                        if i >= 1:
                            gpsimd.wait_ge(sdbgr, 16 * i)
                        gpsimd.dma_start(out=dbg_r1b_h[i],
                                         in_=r1b[:, i % 3]).then_inc(sdbgr, 16)
                if 1 <= s <= NCH:
                    cc = s - 1
                    gpsimd.wait_ge(srwp[cc % 2], 16 * (cc // 2 + 1))
                    gpsimd.collective_compute(
                        "AllGather", ALU.bypass,
                        ins=[a2i_h[cc]],
                        outs=[a2o_h[cc]],
                        replica_groups=[[0, 1], [2, 3], [4, 5], [6, 7]],
                    ).then_inc(sa2a, 1)

        @block.tensor
        def _(tensor):
            tensor.wait_ge(sld, 128)
            tensor.wait_ge(swst, NWT + 1)
            u = ("stats",)
            tensor.matmul(ps_m1[0:2, m1_par[u], 0:1], acc2[:, 0:2], onescol[:, 0:1],
                          start=True, stop=True).then_inc(smms, 1)

            def f16tr(src_ap, key):
                k = f16_idx[key]
                if k % 8 == 0 and k >= 16:
                    tensor.wait_ge(seh, (k - 16) // 8 + 1)
                tensor.transpose(ps_th[:, (k // 8) % 2, k % 8], src_ap,
                                 identh[:]).then_inc(sth, 1)

            def f32tr(src_ap, key):
                k = f32_idx[key]
                if k % 4 == 0 and k >= 8:
                    tensor.wait_ge(sef, (k - 8) // 4 + 1)
                tensor.transpose(ps_tf[:, (k // 4) % 2, k % 4], src_ap,
                                 identf[:]).then_inc(stf, 1)

            def m1_take(u):
                if u in m1_prev:
                    tensor.wait_ge(*m1_prev[u])
                return m1_par[u]

            for j in range(NWT):
                tensor.wait_ge(swq, j + 1)
                for k in range(8 if j < NW1 else 32):
                    f16tr(wqh[:, k * 128:(k + 1) * 128], ("w", j, k))
            for s in range(NSLOT):
                c = s
                if c < NCH:
                    for tt in range(2):
                        g = 2 * c + tt
                        tensor.wait_ge(sxp[g % 2], 16 * (g // 2 + 1))
                        for d8 in range(8):
                            f32tr(xbuf[:, g % 2, d8 * 128:(d8 + 1) * 128],
                                  ("x", c, tt, d8))
                    tensor.wait_ge(sef, f32_idx[("x", c, 1, 7)] // 4 + 1)
                    par = m1_take(("pre", c))
                    for d8 in range(8):
                        mm = tensor.matmul(ps_m1[0:96, par, 0:TC], wcat[:, d8],
                                           xT[:, c % 2, d8],
                                           start=(d8 == 0), stop=(d8 == 7))
                        if d8 == 7:
                            mm.then_inc(smm1, 1)
                    tensor.wait_ge(spre, c + 1)
                    for dt in range(4):
                        par = m1_take(("dlt", c, dt))
                        tensor.matmul(ps_m1[:, par, 0:TC], wdt[:, dt],
                                      preSB[0:64, c % 2],
                                      start=True, stop=True).then_inc(smmd, 1)
                if 1 <= s <= NCH:
                    cc = s - 1
                    tensor.wait_ge(sr1v, cc + 1)
                    for tt in range(2):
                        for dt in range(4):
                            f32tr(r1T[:, dt, tt * 128:(tt + 1) * 128],
                                  ("r", cc, tt, dt))
                i = tile_A(s)
                if i is not None:
                    tensor.wait_ge(sxq, i + 1)
                    for k in range(8):
                        f16tr(xqh[:, k * 128:(k + 1) * 128], ("xq", i, k))
                    tensor.wait_ge(seh, f16_idx[("xq", i, 7)] // 8 + 1)
                    for hc in range(8):
                        par = m1_take(("ffn", i, hc))
                        for dt in range(8):
                            mm = tensor.matmul(ps_m1[:, par, :], xqT[:, dt],
                                               w1T[:, dt, hc * 512:(hc + 1) * 512],
                                               start=(dt == 0), stop=(dt == 7))
                            if dt == 7:
                                mm.then_inc(sm1, 1)
                if i is not None:
                    tensor.wait_ge(shq, i + 1)
                    for k in range(32):
                        f16tr(hqh[:, k * 128:(k + 1) * 128], ("hq", i, k))
                    tensor.wait_ge(seh, f16_idx[("hq", i, 31)] // 8 + 1)
                    for e2 in range(2):
                        if i >= 1:
                            tensor.wait_ge(sr2, i)
                        for ht in range(32):
                            mm = tensor.matmul(ps_m2[:, e2, :], hqT[:, ht],
                                               w2T[:, ht, e2 * 512:(e2 + 1) * 512],
                                               start=(ht == 0), stop=(ht == 31))
                            if ht == 31:
                                mm.then_inc(sm2, 1)

        @block.scalar
        def _(scalar):
            scalar.wait_ge(sld, 128)
            scalar.wait_ge(sprep, 1)
            scalar.activation(aneg[:], aneg[:], AF.Exp).then_inc(sprep, 1)
            scalar.wait_ge(sprep, 3)
            nsfx = 0

            def f16evac_group(k0):
                scalar.wait_ge(sth, k0 + 8)
                bank = (k0 // 8) % 2
                for q in range(8):
                    uq = f16_seq[k0 + q]
                    src = ps_th[:, bank, q]
                    if uq[0] == "w":
                        j, kk = uq[1], uq[2]
                        dst = (w1T[:, kk, j * 128:(j + 1) * 128] if j < NW1
                               else w2T[:, kk, (j - NW1) * 128:(j - NW1 + 1) * 128])
                    elif uq[0] == "xq":
                        dst = xqT[:, uq[2], :]
                    else:
                        dst = hqT[:, uq[2], :]
                    op = scalar.copy(dst, src)
                op.then_inc(seh, 1)

            def f32evac_group(k0):
                scalar.wait_ge(stf, k0 + 4)
                bank = (k0 // 4) % 2
                for q in range(4):
                    uq = f32_seq[k0 + q]
                    src = ps_tf[:, bank, q]
                    if uq[0] == "x":
                        cq, ttq, d8q = uq[1], uq[2], uq[3]
                        dst = xT[:, cq % 2, d8q, ttq * 128:(ttq + 1) * 128]
                    else:
                        ccq, ttq, dtq = uq[1], uq[2], uq[3]
                        dst = r1o[:, ttq, dtq * 128:(dtq + 1) * 128]
                    op = scalar.copy(dst, src)
                op.then_inc(sef, 1)

            nf16 = 0
            for j in range(NWT):
                for k in range(8 if j < NW1 else 32):
                    if nf16 % 8 == 7:
                        f16evac_group(nf16 - 7)
                    nf16 += 1
            nf32 = 0
            for s in range(NSLOT):
                c = s
                if c < NCH:
                    if c >= 2:
                        scalar.wait_ge(smm1, c - 1)
                        scalar.wait_ge(sr1v, c - 1)
                    for tt in range(2):
                        for d8 in range(8):
                            if nf32 % 4 == 3:
                                f32evac_group(nf32 - 3)
                            nf32 += 1
                    for dt in range(4):
                        idx = 4 * c + dt
                        scalar.wait_ge(smmd, idx + 1)
                        spb = xbuf[:, c % 2, dt * TC:(dt + 1) * TC]
                        psl = ps_m1[:, m1_par[("dlt", c, dt)], 0:TC]
                        scalar.activation(spb, psl, AF.Abs,
                                          bias=bdt[:, dt:dt + 1]).then_inc(ssfx, 1)
                        nsfx += 1
                        scalar.wait_ge(ssfx, nsfx)
                        scalar.activation(spb, spb, AF.Exp,
                                          scale=cm1[:, 0:1]).then_inc(ssfx, 1)
                        nsfx += 1
                        scalar.wait_ge(ssfx, nsfx)
                        scalar.activation(spb, spb, AF.Ln, bias=1.0)
                        scalar.activation(delta[:, c % 2, dt], psl, AF.Relu,
                                          bias=bdt[:, dt:dt + 1]).then_inc(ssp, 1)
                i = tile_A(s)
                if 1 <= s <= NCH:
                    ce = s - 1
                    for n in range(2, NST):
                        for dt in range(4):
                            scalar.wait_ge(ssc, 64 * ce + 4 * (n - 2) + dt + 1)
                            scalar.activation(e0[:, (ce + n) % 2, dt],
                                              delta[:, ce % 2, dt], AF.Exp,
                                              scale=aneg[:, dt, n:n + 1]).then_inc(se0, 1)
                if 1 <= s <= NCH:
                    cc = s - 1
                    if cc >= 1:
                        scalar.wait_ge(srwp[(cc - 1) % 2], 16 * ((cc - 1) // 2 + 1))
                    for tt in range(2):
                        for dt in range(4):
                            if nf32 % 4 == 3:
                                f32evac_group(nf32 - 3)
                            nf32 += 1
                if i is not None:
                    if i >= 1:
                        scalar.wait_ge(sm2, 2 * i)
                    for k in range(8):
                        if nf16 % 8 == 7:
                            f16evac_group(nf16 - 7)
                        nf16 += 1
                    scalar.wait_ge(sgsc, i + 1)
                    if i >= 1:
                        scalar.wait_ge(shq, i)
                    for hc in range(8):
                        scalar.wait_ge(sm1, 8 * i + hc + 1)
                        scalar.activation(gb[:, hc * 512:(hc + 1) * 512],
                                          ps_m1[:, m1_par[("ffn", i, hc)], :],
                                          AF.Gelu,
                                          scale=stat[:, i % 4, 5:6]).then_inc(sg, 1)
                if i is not None:
                    for k in range(32):
                        if nf16 % 8 == 7:
                            f16evac_group(nf16 - 7)
                        nf16 += 1
                if c < NCH:
                    for n in range(2):
                        for dt in range(4):
                            scalar.wait_ge(sdel, 4 * c + dt + 1)
                            if c >= 1:
                                npr = 15 if n == 0 else 14
                                scalar.wait_ge(ssc, 64 * (c - 1) + 4 * npr + dt + 1)
                            scalar.activation(e0[:, (c + n) % 2, dt],
                                              delta[:, c % 2, dt], AF.Exp,
                                              scale=aneg[:, dt, n:n + 1]).then_inc(se0, 1)

        @block.vector
        def _(vector):
            vector.memset(cm1[:], -1.0)
            vector.memset(onescol[:], 1.0)
            vector.memset(hstate[:], 0.0).then_inc(sprep, 1)
            vector.wait_ge(sld, 128)
            vector.tensor_copy(hmask[:, 0:1], hmaskf[:, 0:1])
            vector.wait_ge(sprep, 2)
            vector.tensor_scalar(aneg[:], aneg[:], -1.0, None, ALU.mult)
            vector.tensor_scalar(dplus[:], dplus[:], 1.0, None, ALU.add).then_inc(sprep, 1)
            for j in range(NWT):
                vector.wait_ge(swl, 16 * (j + 1))
                wsl = wload[:, 0:D_MODEL] if j < NW1 else wload[:, :]
                col = 0 if j < NW1 else 1
                if j == 0 or j == NW1:
                    vector.reduce_sum(acc2[:, col:col + 1], wsl, axis=AX.X,
                                      apply_absolute_value=True).then_inc(swst, 1)
                else:
                    vector.reduce_sum(sq[:, 0:1], wsl, axis=AX.X,
                                      apply_absolute_value=True)
                    vector.drain()
                    vector.tensor_add(acc2[:, col:col + 1], acc2[:, col:col + 1],
                                      sq[:, 0:1]).then_inc(swst, 1)
            vector.drain().then_inc(swst, 1)
            vector.wait_ge(smms, 1)
            vector.tensor_scalar(svsmall[:, 0:1],
                                 ps_m1[0:2, m1_par[("stats",)], 0:1],
                                 1.0 / (HID * D_MODEL), 1e-5, ALU.mult, ALU.max)
            vector.drain()
            vector.reciprocal(svsmall[:, 1:2], svsmall[:, 0:1])
            vector.drain().then_inc(sscal, 1)
            vector.wait_ge(sbs, 32)
            for j in range(NWT):
                vector.wait_ge(swl, 16 * (NWT + j + 1))
                col = 1 if j < NW1 else 3
                if j >= 1:
                    prev_trs = sum(8 if jj < NW1 else 32 for jj in range(j))
                    vector.wait_ge(sth, prev_trs)
                nchk = 1 if j < NW1 else 4
                for ch in range(nchk):
                    wsl = wload[:, ch * D_MODEL:(ch + 1) * D_MODEL]
                    qsl = wqh[:, ch * D_MODEL:(ch + 1) * D_MODEL]
                    vector.tensor_scalar(sq[:], wsl, sval[:, col:col + 1], None,
                                         ALU.mult)
                    vector.tensor_scalar(qsl, sq[:], 0.5, -1.0, ALU.is_gt, ALU.add)
                    vector.tensor_scalar(xqh[:], sq[:], -0.5, None, ALU.is_ge)
                    vector.tensor_add(qsl, qsl, xqh[:])
                vector.drain().then_inc(swq, 1)
            for s in range(NSLOT):
                c = s
                j = tile_C(s)
                if j is not None:
                    st = stat[:, j % 4]
                    vector.wait_ge(srstd, j + 1)
                    vector.tensor_mul(st[:, 5:6], sval[:, 0:1], st[:, 3:4])
                    vector.drain()
                    vector.tensor_mul(st[:, 5:6], st[:, 5:6], st[:, 2:3])
                    vector.drain()
                    vector.tensor_scalar(st[:, 5:6], st[:, 5:6], 1.0 / 127.0, None,
                                         ALU.mult)
                    vector.drain().then_inc(sgsc, 1)
                    vector.wait_ge(sg, 8 * (j + 1))
                    vector.reduce_max(st[:, 6:7], gb[:], axis=AX.X,
                                      apply_absolute_value=True)
                    vector.drain()
                    vector.tensor_scalar(st[:, 6:7], st[:, 6:7], 1e-5, None, ALU.max)
                    vector.drain()
                    vector.reciprocal(st[:, 7:8], st[:, 6:7])
                    vector.drain()
                    vector.tensor_scalar(st[:, 7:8], st[:, 7:8], 127.0, None,
                                         ALU.mult)
                    vector.drain()
                    vector.tensor_scalar(gb[:], gb[:], st[:, 7:8], 1536.0,
                                         ALU.mult, ALU.add)
                    if j >= 1:
                        vector.wait_ge(sth, f16_idx[("hq", j - 1, 31)] + 1)
                    else:
                        vector.wait_ge(sth, NW1 * 8 + NW2 * 32)
                    vector.tensor_copy(hqh[:], gb[:])
                    vector.tensor_scalar(hqh[:], hqh[:], -1536.0, None, ALU.add)
                    vector.drain().then_inc(shq, 1)
                    vector.tensor_mul(st[:, 8:9], sval[:, 2:3], st[:, 6:7])
                    vector.drain()
                    vector.tensor_scalar(st[:, 8:9], st[:, 8:9], 1.0 / 127.0, None,
                                         ALU.mult)
                    vector.drain()
                k = tile_E(s)
                if k is not None:
                    st = stat[:, k % 4]
                    vector.wait_ge(sm2, 2 * (k + 1))
                    if k >= 1:
                        vector.wait_ge(sowp, 16 * k)
                    vector.tensor_scalar(sq[:], r1b[:, k % 3], st[:, 0:1],
                                         st[:, 2:3], ALU.subtract, ALU.mult)
                    for e2 in range(2):
                        vector.scalar_tensor_tensor(
                            outb[:, e2 * 512:(e2 + 1) * 512],
                            ps_m2[:, e2, :], st[:, 8:9],
                            sq[:, e2 * 512:(e2 + 1) * 512], ALU.mult, ALU.add)
                    vector.reduce_sum(st[:, 9:10], outb[:], axis=AX.X)
                    vector.drain()
                    vector.tensor_scalar(st[:, 9:10], st[:, 9:10], 1.0 / D_MODEL,
                                         None, ALU.mult)
                    vector.drain()
                    vector.tensor_scalar(outb[:], outb[:],
                                         st[:, 9:10], None, ALU.subtract)
                    vector.tensor_mul(sq[:], outb[:], outb[:])
                    vector.reduce_sum(st[:, 9:10], sq[:], axis=AX.X)
                    vector.drain()
                    vector.tensor_scalar(st[:, 9:10], st[:, 9:10], 1.0 / D_MODEL,
                                         EPS, ALU.mult, ALU.add).then_inc(svar2, 1)
                    vector.wait_ge(srstd2, k + 1)
                    vector.tensor_scalar(outb[:], outb[:],
                                         st[:, 10:11], None,
                                         ALU.mult).then_inc(sr2, 1)
                i = tile_A(s)
                if i is not None:
                    st = stat[:, i % 4]
                    vector.wait_ge(sxl[i % 3], 64 * (i // 3 + 1))
                    vector.copy_predicated(r1b[:, i % 3, 0:DH],
                                           hmask[:, 0:1].broadcast_to([128, DH]),
                                           abuf[:, 0, :])
                    vector.copy_predicated(r1b[:, i % 3, DH:D_MODEL],
                                           hmask[:, 0:1].broadcast_to([128, DH]),
                                           abuf[:, 1, :])
                    vector.drain()
                    vector.reduce_sum(st[:, 0:1], r1b[:, i % 3], axis=AX.X)
                    vector.drain()
                    vector.tensor_scalar(st[:, 0:1], st[:, 0:1], 1.0 / D_MODEL, None,
                                         ALU.mult)
                    vector.drain()
                    vector.tensor_scalar(cent[:], r1b[:, i % 3], st[:, 0:1], None,
                                         ALU.subtract)
                    vector.tensor_mul(sq[:], cent[:], cent[:])
                    vector.reduce_sum(st[:, 1:2], sq[:], axis=AX.X)
                    vector.drain()
                    vector.tensor_scalar(st[:, 1:2], st[:, 1:2], 1.0 / D_MODEL, EPS,
                                         ALU.mult, ALU.add).then_inc(svar, 1)
                    vector.reduce_max(st[:, 3:4], cent[:], axis=AX.X,
                                      apply_absolute_value=True)
                    vector.drain()
                    vector.tensor_scalar(st[:, 3:4], st[:, 3:4], 1e-5, None, ALU.max)
                    vector.drain()
                    vector.reciprocal(st[:, 4:5], st[:, 3:4])
                    vector.drain()
                    vector.tensor_scalar(st[:, 4:5], st[:, 4:5], 127.0, None, ALU.mult)
                    vector.drain()
                    vector.tensor_scalar(sq[:], cent[:], st[:, 4:5], 1536.0,
                                         ALU.mult, ALU.add)
                    if i >= 1:
                        vector.wait_ge(sth, f16_idx[("xq", i - 1, 7)] + 1)
                    vector.tensor_copy(xqh[:], sq[:])
                    vector.tensor_scalar(xqh[:], xqh[:], -1536.0, None,
                                         ALU.add)
                    vector.drain().then_inc(sxq, 1)
                if c < NCH:
                    vector.wait_ge(smm1, c + 1)
                    if c >= 2:
                        vector.wait_ge(smmd, 4 * (c - 1))
                        vector.wait_ge(sbw, 16 * (c - 1))
                    vector.tensor_copy(preSB[0:96, c % 2],
                                       ps_m1[0:96, m1_par[("pre", c)], 0:TC]).then_inc(spre, 1)
                if 1 <= s <= NCH:
                    cc = s - 1
                    for dt in range(4):
                        vector.tensor_mul(uu[:, dt], delta[:, cc % 2, dt],
                                          xT[:, cc % 2, dt])
                    vector.wait_ge(sbrp[0], 32 * (8 * cc + 1))
                    vector.tensor_mul(
                        dbx[:, :], uu[:, :],
                        brep[:, 0:1, :].broadcast_to([128, 4, TC])
                    ).then_inc(sdbx, 1)
                    for n in range(NST):
                        p = n % 2
                        for dt in range(4):
                            vector.wait_ge(se0, 64 * cc + 4 * n + dt + 1)
                            vector.tensor_tensor_scan(
                                hbuf[:, dt], e0[:, (cc + n) % 2, dt], dbx[:, dt],
                                hstate[:, dt, n:n + 1], ALU.mult, ALU.add,
                            ).then_inc(ssc, 1)
                        if n == 0:
                            for dt in range(4):
                                vector.tensor_copy(hstate[:, dt, n:n + 1],
                                                   hbuf[:, dt, TC - 1:TC])
                            vector.tensor_mul(
                                yacc[:, :], hbuf[:, :],
                                crep[:, 0:1, :].broadcast_to([128, 4, TC])
                            ).then_inc(scml, 1)
                            np1 = (n + 1) % 2
                            vector.wait_ge(sbrp[np1], 32 * (8 * cc + (n + 1) // 2 + 1))
                            vector.tensor_mul(
                                dbx[:, :], uu[:, :],
                                brep[:, np1:np1 + 1, :].broadcast_to([128, 4, TC])
                            ).then_inc(sdbx, 1)
                        elif n < NST - 1:
                            vector.tensor_mul(
                                tmpy[:, :], hbuf[:, 0:2],
                                crep[:, p:p + 1, :].broadcast_to([128, 2, TC]))
                            for dt in range(4):
                                vector.tensor_copy(hstate[:, dt, n:n + 1],
                                                   hbuf[:, dt, TC - 1:TC])
                            vector.tensor_add(yacc[:, 0:2], yacc[:, 0:2], tmpy[:, :])
                            vector.tensor_mul(
                                tmpy[:, :], hbuf[:, 2:4],
                                crep[:, p:p + 1, :].broadcast_to([128, 2, TC])
                            ).then_inc(scml, 1)
                            np1 = (n + 1) % 2
                            vector.wait_ge(sbrp[np1], 32 * (8 * cc + (n + 1) // 2 + 1))
                            vector.tensor_mul(
                                dbx[:, :], uu[:, :],
                                brep[:, np1:np1 + 1, :].broadcast_to([128, 4, TC])
                            ).then_inc(sdbx, 1)
                            vector.tensor_add(yacc[:, 2:4], yacc[:, 2:4], tmpy[:, :])
                        else:
                            for dt in range(4):
                                vector.tensor_copy(hstate[:, dt, n:n + 1],
                                                   hbuf[:, dt, TC - 1:TC])
                            vector.tensor_mul(tmpy[:, 0], hbuf[:, 0], crep[:, p])
                            vector.tensor_mul(tmpy[:, 1], hbuf[:, 1], crep[:, p])
                            vector.tensor_add(yacc[:, 0], yacc[:, 0], tmpy[:, 0])
                            vector.tensor_mul(tmpy[:, 0], hbuf[:, 2], crep[:, p])
                            vector.tensor_add(yacc[:, 1], yacc[:, 1], tmpy[:, 1])
                            vector.tensor_mul(tmpy[:, 1], hbuf[:, 3], crep[:, p])
                            vector.tensor_add(yacc[:, 2], yacc[:, 2], tmpy[:, 0])
                            vector.tensor_add(yacc[:, 3], yacc[:, 3],
                                              tmpy[:, 1]).then_inc(scml, 1)
                    for dt in range(4):
                        if cc >= 1:
                            vector.wait_ge(stf, f32_idx[("r", cc - 1, 1, 3)] + 1)
                        vector.scalar_tensor_tensor(
                            r1T[:, dt], xT[:, cc % 2, dt], dplus[:, dt:dt + 1],
                            yacc[:, dt], ALU.mult, ALU.add)
                    vector.drain().then_inc(sr1v, 1)
                if c < NCH:
                    for dt in range(4):
                        vector.wait_ge(ssp, 4 * c + dt + 1)
                        if c >= 2:
                            vector.wait_ge(se0, 64 * (c - 1))
                        vector.tensor_add(delta[:, c % 2, dt], delta[:, c % 2, dt],
                                          xbuf[:, c % 2, dt * TC:(dt + 1) * TC]
                                          ).then_inc(sdel, 1)
    return nc


def kernel(L0_attn_x, L0_attn_A_log, L0_attn_W_delta, L0_attn_W_dt, L0_attn_b_dt,
           L0_attn_W_B, L0_attn_W_C, L0_attn_D, L0_ffn_w1, L0_ffn_w2):
    x = np.ascontiguousarray(np.asarray(L0_attn_x, dtype=np.float32))
    alog = np.asarray(L0_attn_A_log, dtype=np.float32)
    wdelta = np.asarray(L0_attn_W_delta, dtype=np.float32)
    wdt = np.asarray(L0_attn_W_dt, dtype=np.float32)
    bdt = np.asarray(L0_attn_b_dt, dtype=np.float32)
    wb = np.asarray(L0_attn_W_B, dtype=np.float32)
    wc = np.asarray(L0_attn_W_C, dtype=np.float32)
    dvec = np.asarray(L0_attn_D, dtype=np.float32)
    w1 = np.ascontiguousarray(np.asarray(L0_ffn_w1, dtype=np.float32))
    w2 = np.ascontiguousarray(np.asarray(L0_ffn_w2, dtype=np.float32))

    wcat = np.ascontiguousarray(np.concatenate([wdelta, wb, wc], axis=1))
    identf = np.eye(128, dtype=np.float32)
    identh = np.eye(128, dtype=np.float16)

    nc = build_fused()
    in_maps = []
    for c in range(8):
        b, h = divmod(c, 2)
        d0 = h * DH
        if h == 0:
            xp = np.ascontiguousarray(x[b])
            wcp = wcat
        else:
            xp = np.ascontiguousarray(np.concatenate([x[b][:, DH:], x[b][:, :DH]], axis=1))
            wcp = np.ascontiguousarray(np.concatenate([wcat[DH:], wcat[:DH]], axis=0))
        in_maps.append({
            "x": xp,
            "hmask_d": np.full((DH,), float(h), np.float32),
            "wcat_d": wcp,
            "wdt_d": np.ascontiguousarray(wdt[:, d0:d0 + DH]),
            "bdt_d": np.ascontiguousarray(bdt[d0:d0 + DH]),
            "alog_d": np.ascontiguousarray(alog[d0:d0 + DH]),
            "dvec_d": np.ascontiguousarray(dvec[d0:d0 + DH]),
            "identf_d": identf,
            "identh_d": identh,
            "w1": w1,
            "w2": w2,
        })
    res = run_bass_kernel_spmd(nc, in_maps, core_ids=list(range(8)))

    out = np.empty((BATCH, SEQ, D_MODEL), dtype=np.float32)
    for c in range(8):
        b, h = divmod(c, 2)
        o = res.results[c]["out"]
        for i in range(NT):
            t0 = i * TC + h * 128
            out[b][t0:t0 + 128] = o[i * 128:(i + 1) * 128]
    return out
